# version 15
# baseline (speedup 1.0000x reference)
import contextlib
import numpy as np
import ml_dtypes
import concourse.bass as bass
import concourse.mybir as mybir
from concourse.bass_utils import run_bass_kernel_spmd

F32 = mybir.dt.float32
BF16 = mybir.dt.bfloat16
FP8 = mybir.dt.float8e4
AX = mybir.AxisListType
AF = mybir.ActivationFunctionType
ALU = mybir.AluOpType

H = 16
DK = 64
D = 1024
TC = 512
NTT = TC // 128
NB = TC // 64
NCORE = 8

RIN, RSK, SPROD = 0, 1, 2

OWN = {"tensor": "te", "vector": "v", "scalar": "s", "sync": None, "gpsimd": None}


class Prog:

    def __init__(self, nc, stack):
        self.nc = nc
        self.items = {e: [] for e in OWN}
        self.ctr = {}
        self.sems = {}
        for name in ("te", "v", "s", "ld", "lf0", "lf1", "gw", "gk",
                     "gb0", "gb1", "gr", "gd", "st0", "st1"):
            self.sems[name] = stack.enter_context(nc.semaphore(name))
            self.ctr[name] = 0

    def add(self, eng, fn, waits=(), inc=None, n=1, selfw=False):
        w = [(s, v) for (s, v) in waits if v > 0]
        if inc is None and OWN[eng] in ("v", "s"):
            inc = OWN[eng]
        self.items[eng].append((w, fn, inc, n, selfw))
        if inc is not None:
            self.ctr[inc] += n
            return self.ctr[inc]
        return None

    def replay(self, engname, engobj):
        waited = {}
        own = OWN[engname]
        for waits, fn, inc, n, selfw in self.items[engname]:
            for sname, val in waits:
                if sname == own and not selfw:
                    continue
                if waited.get(sname, 0) >= val:
                    continue
                engobj.wait_ge(self.sems[sname], val)
                waited[sname] = val
            ins = fn(engobj)
            if inc is not None:
                ins.then_inc(self.sems[inc], n)


def build_program(T, dbg=False):
    NCH = T // TC
    nc = bass.Bass("TRN2", target_bir_lowering=False, debug=False)

    x_h = nc.dram_tensor("x", [T, D], F32, kind="ExternalInput").ap()
    wqkv_h = nc.dram_tensor("wqkv", [8, 128, 3072], F32, kind="ExternalInput").ap()
    wout_h = nc.dram_tensor("wout", [8, 128, 1024], F32, kind="ExternalInput").ap()
    kern_h = nc.dram_tensor("kern", [8, 128, 1024], BF16, kind="ExternalInput").ap()
    bqkv_h = nc.dram_tensor("bqkv", [128, 24], F32, kind="ExternalInput").ap()
    bout_h = nc.dram_tensor("bout", [128, 8], F32, kind="ExternalInput").ap()
    bffn_h = nc.dram_tensor("bffn", [128, 8], BF16, kind="ExternalInput").ap()
    sin_h = nc.dram_tensor("sin", [1, 1], F32, kind="ExternalInput").ap()
    sk_h = nc.dram_tensor("sk", [1, 1], F32, kind="ExternalInput").ap()
    ident_h = nc.dram_tensor("ident", [128, 128], BF16, kind="ExternalInput").ap()
    p8_h = nc.dram_tensor("p8c", [8, 128], BF16, kind="ExternalInput").ap()
    ones_h = nc.dram_tensor("ones1", [1, 128], F32, kind="ExternalInput").ap()
    y_h = nc.dram_tensor("y", [T, D], F32, kind="ExternalOutput").ap()

    dbg_h = {}
    if dbg:
        for name, shape, dt in [
            ("d_q", [64, 8192], BF16), ("d_k", [128, 8192], BF16),
            ("d_v", [128, 8192], BF16), ("d_e", [128, 1024], BF16),
            ("d_den", [128, 8], F32), ("d_at", [128, 1024], BF16),
            ("d_vt", [128, 1024], BF16), ("d_ctx", [128, 4096], BF16),
            ("d_x1", [128, 4096], F32), ("d_x1n", [128, 4096], F32),
            ("d_xq", [128, 4096], FP8), ("d_yb", [128, 4096], BF16),
            ("d_an", [128, 1024], BF16), ("d_stat", [128, 88], F32),
            ("d_xt", [128, 4096], BF16),
        ]:
            dbg_h[name] = nc.dram_tensor(name, shape, dt, kind="ExternalOutput").ap()

    st = contextlib.ExitStack()
    en = st.enter_context
    with st:
        swq = en(nc.sbuf_tensor("swq", [128, 24576], BF16))
        swo = en(nc.sbuf_tensor("swo", [128, 8192], BF16))
        skq = en(nc.sbuf_tensor("skq", [128, 8192], FP8))
        sbq = en(nc.sbuf_tensor("sbq", [128, 24], F32))
        sbo = en(nc.sbuf_tensor("sbo", [128, 8], F32))
        sbfb = en(nc.sbuf_tensor("sbfb", [128, 8], BF16))
        sbf32 = en(nc.sbuf_tensor("sbf32", [128, 8], F32))
        sid = en(nc.sbuf_tensor("sid", [128, 128], BF16))
        sp8 = en(nc.sbuf_tensor("sp8", [8, 128], BF16))
        sones = en(nc.sbuf_tensor("sones", [1, 128], F32))
        ssc = en(nc.sbuf_tensor("ssc", [1, 4], F32))
        ssc2 = en(nc.sbuf_tensor("ssc2", [1, 4], F32))
        srcp = en(nc.sbuf_tensor("srcp", [128, 4], F32))
        negc = en(nc.sbuf_tensor("negc", [128, 1], F32))
        qbuf = en(nc.sbuf_tensor("qbuf", [64, 8192], BF16))
        kbuf = en(nc.sbuf_tensor("kbuf", [128, 8192], BF16))
        vbuf = en(nc.sbuf_tensor("vbuf", [128, 8192], BF16))
        xT = en(nc.sbuf_tensor("xT", [128, 4096], BF16))
        xb = [en(nc.sbuf_tensor(f"xb{i}", [128, 1024], BF16)) for i in range(2)]
        xft = [en(nc.sbuf_tensor(f"xft{i}", [128, 1024], F32)) for i in range(2)]
        x1t = [en(nc.sbuf_tensor(f"x1t{i}", [128, 1024], F32)) for i in range(2)]
        sE = en(nc.sbuf_tensor("sE", [128, 1024], BF16))
        sAttn = en(nc.sbuf_tensor("sAttn", [128, 1024], BF16))
        sAT = en(nc.sbuf_tensor("sAT", [128, 1024], BF16))
        sVT = en(nc.sbuf_tensor("sVT", [128, 1024], BF16))
        sden = en(nc.sbuf_tensor("sden", [128, 8], F32))
        sdrc = en(nc.sbuf_tensor("sdrc", [128, 8], F32))
        ctx = en(nc.sbuf_tensor("ctx", [128, 4096], BF16))
        aoTc = en(nc.sbuf_tensor("aoTc", [128, 4096], BF16))
        x1n = en(nc.sbuf_tensor("x1n", [128, 4096], F32))
        x1nb = en(nc.sbuf_tensor("x1nb", [128, 1024], BF16))
        xqT = en(nc.sbuf_tensor("xqT", [128, 4096], FP8))
        ybT = en(nc.sbuf_tensor("ybT", [128, 4096], BF16))
        sstat = en(nc.sbuf_tensor("sstat", [128, 88], F32))

        pb2a = en(nc.psum_tensor("pb2a", [128, 1024], F32))
        pb2b = en(nc.psum_tensor("pb2b", [128, 1024], F32))
        pb2c = en(nc.psum_tensor("pb2c", [128, 1024], F32))
        pb1a = en(nc.psum_tensor("pb1a", [128, 512], F32))
        pb1b = en(nc.psum_tensor("pb1b", [128, 512], F32))

        P = Prog(nc, st)

        bguard = {}

        def bset(name, sem, val):
            bguard[name] = [(sem, val)]

        def badd(name, sem, val):
            bguard.setdefault(name, []).append((sem, val))

        def bg(*names):
            out = []
            for nm in names:
                out += bguard.get(nm, [])
            return out

        def vdrain(inc=None):
            return P.add("vector", lambda e: e.drain(),
                         waits=[("v", P.ctr["v"])], selfw=True, inc=inc)

        def sdrain(inc=None):
            return P.add("scalar", lambda e: e.drain(),
                         waits=[("s", P.ctr["s"])], selfw=True, inc=inc)

        P.add("sync", lambda e: e.dma_start(out=sid[:], in_=ident_h[:]), inc="ld", n=16)
        P.add("sync", lambda e: e.dma_start(out=sp8[:], in_=p8_h[:]), inc="ld", n=16)
        P.add("sync", lambda e: e.dma_start(out=sones[:], in_=ones_h[:]), inc="ld", n=16)
        P.add("sync", lambda e: e.dma_start(out=sbq[:], in_=bqkv_h[:]), inc="ld", n=16)
        P.add("sync", lambda e: e.dma_start(out=sbo[:], in_=bout_h[:]), inc="ld", n=16)
        P.add("sync", lambda e: e.dma_start(out=sbfb[:], in_=bffn_h[:]), inc="ld", n=16)
        P.add("sync", lambda e: e.dma_start(out=ssc[0:1, 0:1], in_=sin_h[:]), inc="ld", n=16)
        ld_consts = P.add("sync", lambda e: e.dma_start(out=ssc[0:1, 1:2], in_=sk_h[:]),
                          inc="ld", n=16)

        P.add("gpsimd", lambda e: e.dma_start(
            out=swq[:].rearrange("p (c f) -> p c f", c=8), in_=wqkv_h.rearrange("c p f -> p c f")), inc="gw", n=16)
        gp_w = P.add("gpsimd", lambda e: e.dma_start(
            out=swo[:].rearrange("p (j f) -> p j f", j=8), in_=wout_h.rearrange("j p f -> p j f")), inc="gw", n=16)

        P.add("vector", lambda e: e.tensor_copy(sbf32[:], sbfb[:]),
              waits=[("ld", ld_consts)], inc="v")
        P.add("vector", lambda e: e.reciprocal(ssc2[0:1, 0:2], ssc[0:1, 0:2]), inc="v")
        P.add("vector", lambda e: e.tensor_mul(
            ssc2[0:1, 2:3], ssc[0:1, 0:1], ssc[0:1, 1:2]), inc="v")
        v_rc = vdrain(inc="v")
        for k in range(3):
            P.add("tensor", lambda e, k=k: e.matmul(
                pb1a[:, k:k + 1], sones[0:1, :], ssc2[0:1, k:k + 1], start=True, stop=True),
                waits=[("v", v_rc)], inc="te" if k == 2 else None)
        te_bc = P.ctr["te"]
        P.add("vector", lambda e: e.tensor_copy(srcp[:, 0:3], pb1a[:, 0:3]),
              waits=[("te", te_bc)], inc="v")
        v_srcp = vdrain(inc="v")
        bset("pb1a", "v", v_srcp)
        P.add("vector", lambda e: e.memset(vbuf[:], 0.0), inc="v")
        v_vz = vdrain(inc="v")

        s_q = 0
        for cc in range(8):
            ldv = P.add("gpsimd", lambda e, cc=cc: e.dma_start(out=x1nb[:], in_=kern_h[cc]),
                        waits=[("s", s_q)], inc="gk", n=16)
            s_q = P.add("scalar", lambda e, cc=cc: e.activation(
                skq[:, cc * 1024:(cc + 1) * 1024], x1nb[:], AF.Identity, bias=0.0,
                scale=srcp[:, RSK:RSK + 1]),
                waits=[("gk", ldv), ("v", v_srcp)], inc="s")
        bset("x1nb", "s", s_q)

        def q_dest(h):
            return qbuf[0:64, :].rearrange("p (G u) -> p G u", G=64)[:, :, h * 8:(h + 1) * 8]

        def v_dest(g):
            rh = g % 2
            return vbuf[rh * 64:(rh + 1) * 64, :].rearrange(
                "p (G u) -> p G u", G=64)[:, :, g * 8:(g + 1) * 8]

        v_negc = 0
        s_prev_apply1 = [0, 0]
        s_prev_apply2 = [0, 0]
        v_xft_guard = [("v", 0), ("v", 0)]
        x1t_st_guard = [0, 0]
        st_out = 0

        for c in range(NCH):
            v_xT_done = 0
            for tt in range(NTT):
                ldb = P.add("gpsimd", lambda e, c=c, tt=tt: e.dma_start(
                    out=xb[tt % 2][:],
                    in_=x_h.rearrange("(n p) d -> n p d", p=128)[c * NTT + tt]),
                    waits=bg(("xb", tt % 2)), inc="gb%d" % (tt % 2), n=16)
                bank = ["pb2a", "pb2b"][tt % 2]
                pbx = [pb2a, pb2b][tt % 2]
                for ec in range(8):
                    P.add("tensor", lambda e, tt=tt, ec=ec, pbx=pbx: e.matmul(
                        pbx[:, ec * 128:(ec + 1) * 128],
                        xb[tt % 2][:, ec * 128:(ec + 1) * 128], sid[:],
                        start=True, stop=True),
                        waits=([("gb%d" % (tt % 2), ldb), ("ld", ld_consts)] + bg(bank))
                        if ec == 0 else [],
                        inc="te" if ec == 7 else None)
                tev = P.ctr["te"]
                bset(("xb", tt % 2), "te", tev)
                for hf in range(2):
                    v_xT_done = P.add("vector", lambda e, tt=tt, hf=hf, pbx=pbx: e.tensor_copy(
                        xT[:].rearrange("p (ec u) -> p ec u", ec=8)[
                            :, hf * 4:(hf + 1) * 4, tt * 128:(tt + 1) * 128],
                        pbx[:, hf * 512:(hf + 1) * 512].rearrange("p (a u) -> p a u", a=4)),
                        waits=[("te", tev)], inc="v")
                bset(bank, "v", v_xT_done)

            v_evac = 0
            s_evac = 0
            for oc in range(24):
                bank = ["pb1a", "pb1b"][oc % 2]
                pbq = [pb1a, pb1b][oc % 2]
                for kc in range(8):
                    P.add("tensor", lambda e, oc=oc, kc=kc, pbq=pbq: e.matmul(
                        pbq[:], swq[:, kc * 3072 + oc * 128: kc * 3072 + (oc + 1) * 128],
                        xT[:, kc * 512:(kc + 1) * 512],
                        start=(kc == 0), stop=(kc == 7)),
                        waits=([("v", v_xT_done), ("gw", gp_w)] + bg(bank))
                        if kc == 0 else [],
                        inc="te" if kc == 7 else None)
                tev = P.ctr["te"]
                guards = []
                if oc < 16:
                    hq = oc
                    v_evac = P.add("vector", lambda e, oc=oc, hq=hq, pbq=pbq: e.tensor_scalar(
                        q_dest(hq),
                        pbq[0:64, :].rearrange("p (G t) -> p G t", G=64),
                        sbq[0:64, oc:oc + 1], 0.125, ALU.add, ALU.mult),
                        waits=[("te", tev)], inc="v")
                    guards.append(("v", v_evac))
                    gk = oc
                    if oc < 8:
                        v_evac = P.add("vector", lambda e, oc=oc, gk=gk, pbq=pbq: e.tensor_scalar(
                            kbuf[64:128, gk * 512:(gk + 1) * 512], pbq[64:128, :],
                            sbq[64:128, oc:oc + 1], None, ALU.add),
                            waits=[("te", tev)], inc="v")
                        guards.append(("v", v_evac))
                    else:
                        s_evac = P.add("scalar", lambda e, oc=oc, gk=gk, pbq=pbq: e.activation(
                            kbuf[64:128, gk * 512:(gk + 1) * 512], pbq[64:128, :],
                            AF.Identity, bias=sbq[64:128, oc:oc + 1], scale=1.0),
                            waits=[("te", tev)], inc="s")
                        guards.append(("s", s_evac))
                else:
                    i = oc - 16
                    for hf, g in ((0, 2 * i), (1, 2 * i + 1)):
                        s_evac = P.add("scalar", lambda e, hf=hf, g=g, oc=oc, pbq=pbq: e.activation(
                            v_dest(g),
                            pbq[hf * 64:(hf + 1) * 64, :].rearrange("p (G t) -> p G t", G=64),
                            AF.Identity, bias=sbq[hf * 64:(hf + 1) * 64, oc:oc + 1], scale=1.0),
                            waits=[("te", tev), ("v", v_vz)], inc="s")
                        guards.append(("s", s_evac))
                bguard[bank] = guards
            v_qkv_evac = vdrain(inc="v")
            s_qkv_evac = sdrain(inc="s")
            gp_rep = P.add("gpsimd", lambda e: e.dma_start(
                out=kbuf[0:64, :], in_=kbuf[64:128, :]),
                waits=[("v", v_qkv_evac), ("s", s_qkv_evac)], inc="gr", n=16)

            if dbg and c == 0:
                P.add("gpsimd", lambda e: e.dma_start(out=dbg_h["d_xt"][:], in_=xT[:]),
                      waits=[("v", v_xT_done)], inc="gd", n=16)
                P.add("gpsimd", lambda e: e.dma_start(out=dbg_h["d_q"][:], in_=qbuf[:]),
                      waits=[("v", v_qkv_evac), ("s", s_qkv_evac)], inc="gd", n=16)
                P.add("gpsimd", lambda e: e.dma_start(out=dbg_h["d_k"][:], in_=kbuf[:]),
                      waits=[("gr", gp_rep)], inc="gd", n=16)
                P.add("gpsimd", lambda e: e.dma_start(out=dbg_h["d_v"][:], in_=vbuf[:]),
                      waits=[("v", v_qkv_evac), ("s", s_qkv_evac)], inc="gd", n=16)

            s_exp = 0
            v_attn = 0
            te_at = 0
            te_mm3 = 0
            s_ctx_evac = 0
            v_dbg_den = 0
            gp_dbg_atvt = 0
            for b in range(NB):
                v_dbg_den = v_dbg_den if b else 0
                for gl in range(8):
                    G = b * 8 + gl
                    P.add("tensor", lambda e, gl=gl: e.matmul(
                        pb2a[:, gl * 128:(gl + 1) * 128], sp8[0:8, :], sp8[0:8, :],
                        start=True, stop=False),
                        waits=([("v", v_qkv_evac), ("s", s_qkv_evac), ("gr", gp_rep)]
                               + bg("pb2a")) if gl == 0 else [])
                    P.add("tensor", lambda e, G=G, gl=gl: e.matmul(
                        pb2a[:, gl * 128:(gl + 1) * 128],
                        qbuf[:, G * 128:(G + 1) * 128],
                        kbuf[0:64, :].rearrange("p (g u) -> p g u", g=16)[
                            :, :, G * 8:(G + 1) * 8],
                        start=False, stop=True),
                        inc="te" if gl == 7 else None)
                te_mm1 = P.ctr["te"]
                if c == 0 and b == 0:
                    P.add("vector", lambda e: e.reduce_max(
                        negc[:, 0:1], pb2a[:], axis=AX.X, negate=True),
                        waits=[("te", te_mm1)], inc="v")
                    v_negc = vdrain(inc="v")
                s_exp = P.add("scalar", lambda e: e.activation(
                    sE[:], pb2a[:], AF.Exp, bias=negc[:, 0:1], scale=1.0),
                    waits=[("te", te_mm1), ("v", v_negc), ("v", v_attn)],
                    inc="s")
                bset("pb2a", "s", s_exp)
                gp_dbg = 0
                if dbg and c == 0 and b == 0:
                    gp_dbg = P.add("gpsimd", lambda e: e.dma_start(out=dbg_h["d_e"][:], in_=sE[:]),
                                   waits=[("s", s_exp)], inc="gd", n=16)
                P.add("vector", lambda e: e.reduce_sum(
                    sden[:, 0:8], sE[:].rearrange("p (g u) -> p g u", g=8), axis=AX.X),
                    waits=[("s", s_exp), ("gd", v_dbg_den), ("gd", gp_dbg)], inc="v")
                if dbg and c == 0 and b == 0:
                    vden = P.ctr["v"]
                    gp_dbg = P.add("gpsimd", lambda e: e.dma_start(out=dbg_h["d_den"][:], in_=sden[:]),
                                   waits=[("v", vden)], inc="gd", n=16)
                    v_dbg_den_next = gp_dbg
                else:
                    v_dbg_den_next = 0
                v_dbg_den = v_dbg_den_next
                P.add("vector", lambda e: e.drain())
                P.add("vector", lambda e: e.reciprocal(sdrc[:, 0:8], sden[:, 0:8]), inc="v")
                P.add("vector", lambda e: e.drain())
                v_attn = P.add("vector", lambda e: e.tensor_tensor(
                    sAttn[:].rearrange("p (g u) -> p g u", g=8),
                    sE[:].rearrange("p (g u) -> p g u", g=8),
                    sdrc[:, 0:8].rearrange("p (g o) -> p g o", o=1).broadcast_to([128, 8, 128]),
                    ALU.mult), waits=[("gd", gp_dbg), ("te", te_at)], inc="v")
                gp_dbg_an = 0
                if dbg and c == 0 and b == 0:
                    gp_dbg_an = P.add("gpsimd", lambda e: e.dma_start(
                        out=dbg_h["d_an"][:], in_=sAttn[:]),
                        waits=[("v", v_attn)], inc="gd", n=16)
                for gl in range(8):
                    G = b * 8 + gl
                    P.add("tensor", lambda e, G=G, gl=gl: e.matmul(
                        pb2c[:, gl * 128:(gl + 1) * 128],
                        vbuf[:, G * 128:(G + 1) * 128], sid[:],
                        start=True, stop=True),
                        waits=([("v", v_qkv_evac), ("s", s_qkv_evac)] + bg("pb2c"))
                        if gl == 0 else [],
                        inc="te" if gl == 7 else None)
                te_vt = P.ctr["te"]
                v_vt = P.add("vector", lambda e: e.tensor_copy(
                    sVT[:, 0:512], pb2c[:, 0:512]),
                    waits=[("te", te_vt), ("te", te_mm3), ("gd", gp_dbg_atvt)], inc="v")
                s_vt = P.add("scalar", lambda e: e.copy(
                    sVT[:, 512:1024], pb2c[:, 512:1024]),
                    waits=[("te", te_vt), ("te", te_mm3), ("gd", gp_dbg_atvt)], inc="s")
                bguard["pb2c"] = [("v", v_vt), ("s", s_vt)]
                for gl in range(8):
                    P.add("tensor", lambda e, gl=gl: e.matmul(
                        pb2b[:, gl * 128:(gl + 1) * 128],
                        sAttn[:, gl * 128:(gl + 1) * 128], sid[:],
                        start=True, stop=True),
                        waits=([("v", v_attn), ("gd", gp_dbg_an)] + bg("pb2b"))
                        if gl == 0 else [],
                        inc="te" if gl == 7 else None)
                te_at = P.ctr["te"]
                v_at = P.add("vector", lambda e: e.tensor_copy(
                    sAT[:, 0:512], pb2b[:, 0:512]),
                    waits=[("te", te_at), ("te", te_mm3), ("gd", gp_dbg_atvt)], inc="v")
                s_at = P.add("scalar", lambda e: e.copy(sAT[:, 512:1024], pb2b[:, 512:1024]),
                             waits=[("te", te_at), ("te", te_mm3), ("gd", gp_dbg_atvt)],
                             inc="s")
                bguard["pb2b"] = [("v", v_at), ("s", s_at)]
                if dbg and c == 0 and b == 0:
                    P.add("gpsimd", lambda e: e.dma_start(out=dbg_h["d_at"][:], in_=sAT[:]),
                          waits=[("v", v_at)], inc="gd", n=16)
                    gp_dbg_atvt = P.add("gpsimd", lambda e: e.dma_start(
                        out=dbg_h["d_vt"][:], in_=sVT[:]),
                        waits=[("s", s_vt)], inc="gd", n=16)
                for gl in range(8):
                    for bb in range(2):
                        for pp in range(2):
                            P.add("tensor", lambda e, gl=gl, bb=bb, pp=pp: e.matmul(
                                pb1a[bb * 64:(bb + 1) * 64, gl * 64:(gl + 1) * 64],
                                sVT[:, gl * 128 + pp * 64: gl * 128 + (pp + 1) * 64],
                                sAT[:, gl * 128 + bb * 64: gl * 128 + (bb + 1) * 64],
                                start=(pp == 0), stop=(pp == 1),
                                tile_position=(0, bb * 64)),
                                waits=([("v", v_at), ("s", s_at), ("v", v_vt),
                                        ("s", s_vt)] + bg("pb1a"))
                                if (gl == 0 and bb == 0 and pp == 0) else [],
                                inc="te" if (gl == 7 and bb == 1 and pp == 1) else None)
                te_mm3 = P.ctr["te"]
                if b % 2 == 0:
                    v_ctx = P.add("vector", lambda e, b=b: e.tensor_copy(
                        ctx[:, b * 512:(b + 1) * 512], pb1a[:]),
                        waits=[("te", te_mm3)], inc="v")
                    bguard["pb1a"] = [("v", v_ctx)]
                else:
                    s_ctx_evac = P.add("scalar", lambda e, b=b: e.copy(
                        ctx[:, b * 512:(b + 1) * 512], pb1a[:]),
                        waits=[("te", te_mm3)], inc="s")
                    bguard["pb1a"] = [("s", s_ctx_evac)]
            if dbg and c == 0:
                P.add("gpsimd", lambda e: e.dma_start(out=dbg_h["d_ctx"][:], in_=ctx[:]),
                      waits=[("s", s_ctx_evac)], inc="gd", n=16)

            s_ao = 0
            for oc in range(8):
                bank = ["pb1b", "pb1a"][oc % 2]
                pbw = [pb1b, pb1a][oc % 2]
                for j in range(8):
                    P.add("tensor", lambda e, oc=oc, j=j, pbw=pbw: e.matmul(
                        pbw[:], swo[:, j * 1024 + oc * 128: j * 1024 + (oc + 1) * 128],
                        ctx[:].rearrange("p (G u) -> p G u", G=64)[:, :, j * 8:(j + 1) * 8],
                        start=(j == 0), stop=(j == 7)),
                        waits=([("s", s_ctx_evac), ("v", v_ctx)] + bg(bank))
                        if j == 0 else [],
                        inc="te" if j == 7 else None)
                tev = P.ctr["te"]
                s_ao = P.add("scalar", lambda e, oc=oc, pbw=pbw: e.activation(
                    aoTc[:, oc * 512:(oc + 1) * 512], pbw[:], AF.Identity,
                    bias=sbo[:, oc:oc + 1], scale=1.0),
                    waits=[("te", tev)], inc="s")
                bset(bank, "s", s_ao)

            s_quant = 0
            for pr in range(2):
                base = pr * 22
                h0s = sstat[:, base:base + 4].rearrange("p (t h) -> p t h", t=2)[:, :, 0:1]
                h1s = sstat[:, base:base + 4].rearrange("p (t h) -> p t h", t=2)[:, :, 1:2]
                ssq2 = sstat[:, base + 4:base + 6]
                s12 = sstat[:, base + 6:base + 8]
                m2 = sstat[:, base + 8:base + 10]
                t12 = sstat[:, base + 10:base + 12]
                nm22 = sstat[:, base + 12:base + 14]
                ve2 = sstat[:, base + 14:base + 16]
                std2 = sstat[:, base + 16:base + 18]
                rs2 = sstat[:, base + 18:base + 20]
                nmb2 = sstat[:, base + 20:base + 22]
                for ti in range(2):
                    tt = 2 * pr + ti
                    par = tt % 2
                    ldf = P.add("sync", lambda e, c=c, tt=tt, par=par: e.dma_start(
                        out=xft[par][:],
                        in_=x_h.rearrange("(n p) d -> n p d", p=128)[c * NTT + tt]),
                        waits=[v_xft_guard[par]], inc="lf%d" % par, n=16)
                    for oc in range(8):
                        P.add("tensor", lambda e, tt=tt, oc=oc: e.matmul(
                            pb2a[:, oc * 128:(oc + 1) * 128],
                            aoTc[:, oc * 512 + tt * 128: oc * 512 + (tt + 1) * 128], sid[:],
                            start=True, stop=True),
                            waits=([("s", s_ao)] + bg("pb2a")) if oc == 0 else [],
                            inc="te" if oc == 7 else None)
                    tev = P.ctr["te"]
                    v_ev2 = 0
                    for hf in range(2):
                        acol = sstat[:, base + ti * 2 + hf: base + ti * 2 + hf + 1]
                        v_ev2 = P.add("vector", lambda e, hf=hf, par=par, acol=acol: e.scalar_tensor_tensor(
                            x1t[par][:].rearrange("p (o u) -> p o u", o=8)[
                                :, hf * 4:(hf + 1) * 4, :],
                            pb2a[:, hf * 512:(hf + 1) * 512].rearrange("p (a u) -> p a u", a=4),
                            1.0,
                            xft[par][:].rearrange("p (o u) -> p o u", o=8)[
                                :, hf * 4:(hf + 1) * 4, :],
                            ALU.mult, ALU.add, accum_out=acol),
                            waits=[("te", tev), ("lf%d" % par, ldf),
                                   ("s", s_prev_apply1[par]),
                                   ("st%d" % par, x1t_st_guard[par])],
                            inc="v")
                    bset("pb2a", "v", v_ev2)
                    v_xft_guard[par] = ("v", v_ev2)
                    vdrain()
                    P.add("vector", lambda e, par=par, ti=ti, ssq2=ssq2: e.scalar_tensor_tensor(
                        x1nb[:], x1t[par][:], 1.0, x1t[par][:], ALU.mult, ALU.mult,
                        accum_out=ssq2[:, ti:ti + 1]),
                        waits=bg("x1nb"), inc="v")
                vdrain()
                P.add("vector", lambda e, s12=s12, h0s=h0s, h1s=h1s: e.tensor_tensor(
                    s12, h0s, h1s, ALU.add), inc="v")
                P.add("vector", lambda e, t12=t12, ssq2=ssq2: e.tensor_scalar(
                    t12, ssq2, D * 1e-5, 1.0 / D, ALU.add, ALU.mult), inc="v")
                vdrain()
                P.add("vector", lambda e, m2=m2, s12=s12: e.tensor_scalar(
                    m2, s12, 1.0 / D, None, ALU.mult), inc="v")
                vdrain()
                P.add("vector", lambda e, nm22=nm22, m2=m2: e.scalar_tensor_tensor(
                    nm22, m2, -1.0, m2, ALU.mult, ALU.mult), inc="v")
                vdrain()
                P.add("vector", lambda e, ve2=ve2, t12=t12, nm22=nm22: e.tensor_tensor(
                    ve2, t12, nm22, ALU.add), inc="v")
                vv = vdrain(inc="v")
                P.add("scalar", lambda e, std2=std2, ve2=ve2: e.activation(
                    std2, ve2, AF.Sqrt, bias=0.0, scale=1.0),
                    waits=[("v", vv)], inc="s")
                sq = sdrain(inc="s")
                P.add("vector", lambda e, rs2=rs2, std2=std2: e.reciprocal(rs2, std2),
                      waits=[("s", sq)], inc="v")
                vdrain()
                P.add("vector", lambda e, nmb2=nmb2, m2=m2, rs2=rs2: e.scalar_tensor_tensor(
                    nmb2, m2, -1.0, rs2, ALU.mult, ALU.mult), inc="v")
                vnm = vdrain(inc="v")
                for ti in range(2):
                    tt = 2 * pr + ti
                    par = tt % 2
                    rsc = rs2[:, ti:ti + 1]
                    nmbc = nmb2[:, ti:ti + 1]
                    s_ap = P.add("scalar", lambda e, tt=tt, par=par, rsc=rsc, nmbc=nmbc: e.activation(
                        x1n[:, tt * 1024:(tt + 1) * 1024], x1t[par][:],
                        AF.Identity, bias=nmbc, scale=rsc),
                        waits=[("v", vnm)], inc="s")
                    s_prev_apply1[par] = s_ap
                    vcb = P.add("vector", lambda e, tt=tt: e.tensor_copy(
                        x1nb[:], x1n[:, tt * 1024:(tt + 1) * 1024]),
                        waits=[("s", s_ap)] + bg("x1nb"), inc="v")
                    for ec in range(8):
                        P.add("tensor", lambda e, ec=ec: e.matmul(
                            pb2b[:, ec * 128:(ec + 1) * 128],
                            x1nb[:, ec * 128:(ec + 1) * 128], sid[:],
                            start=True, stop=True),
                            waits=([("v", vcb)] + bg("pb2b")) if ec == 0 else [],
                            inc="te" if ec == 7 else None)
                    tevq = P.ctr["te"]
                    bset("x1nb", "te", tevq)
                    for hf in range(2):
                        s_quant = P.add("scalar", lambda e, tt=tt, hf=hf: e.activation(
                            xqT[:].rearrange("p (ec u) -> p ec u", ec=8)[
                                :, hf * 4:(hf + 1) * 4, tt * 128:(tt + 1) * 128],
                            pb2b[:, hf * 512:(hf + 1) * 512].rearrange("p (a u) -> p a u", a=4),
                            AF.Identity, bias=0.0, scale=srcp[:, RIN:RIN + 1]),
                            waits=[("te", tevq)], inc="s")
                    bset("pb2b", "s", s_quant)
            if dbg and c == 0:
                P.add("gpsimd", lambda e: e.dma_start(out=dbg_h["d_stat"][:], in_=sstat[:]),
                      waits=[("s", P.ctr["s"]), ("v", P.ctr["v"])], inc="gd", n=16)
                P.add("gpsimd", lambda e: e.dma_start(out=dbg_h["d_x1n"][:], in_=x1n[:]),
                      waits=[("s", P.ctr["s"])], inc="gd", n=16)
                P.add("gpsimd", lambda e: e.dma_start(out=dbg_h["d_xq"][:], in_=xqT[:]),
                      waits=[("s", s_quant)], inc="gd", n=16)
                P.add("gpsimd", lambda e: e.dma_start(out=dbg_h["d_x1"][:], in_=x1n[:]),
                      waits=[("s", P.ctr["s"])], inc="gd", n=16)

            s_yb = 0
            for oc in range(8):
                bank = ["pb1a", "pb1b"][oc % 2]
                pbf = [pb1a, pb1b][oc % 2]
                for ec in range(8):
                    P.add("tensor", lambda e, oc=oc, ec=ec, pbf=pbf: e.matmul(
                        pbf[:], skq[:, ec * 1024 + oc * 128: ec * 1024 + (oc + 1) * 128],
                        xqT[:, ec * 512:(ec + 1) * 512],
                        start=(ec == 0), stop=(ec == 7)),
                        waits=([("s", s_quant)] + bg(bank)) if ec == 0 else [],
                        inc="te" if ec == 7 else None)
                tev = P.ctr["te"]
                s_yb = P.add("scalar", lambda e, oc=oc, pbf=pbf: e.activation(
                    ybT[:, oc * 512:(oc + 1) * 512], pbf[:], AF.Identity,
                    bias=sbf32[:, oc:oc + 1], scale=srcp[:, SPROD:SPROD + 1]),
                    waits=[("te", tev)], inc="s")
                bset(bank, "s", s_yb)
            if dbg and c == 0:
                P.add("gpsimd", lambda e: e.dma_start(out=dbg_h["d_yb"][:], in_=ybT[:]),
                      waits=[("s", s_yb)], inc="gd", n=16)

            for pr in range(2):
                base = 44 + pr * 22
                h0s = sstat[:, base:base + 4].rearrange("p (t h) -> p t h", t=2)[:, :, 0:1]
                h1s = sstat[:, base:base + 4].rearrange("p (t h) -> p t h", t=2)[:, :, 1:2]
                ssq2 = sstat[:, base + 4:base + 6]
                s12 = sstat[:, base + 6:base + 8]
                m2 = sstat[:, base + 8:base + 10]
                t12 = sstat[:, base + 10:base + 12]
                nm22 = sstat[:, base + 12:base + 14]
                ve2 = sstat[:, base + 14:base + 16]
                std2 = sstat[:, base + 16:base + 18]
                rs2 = sstat[:, base + 18:base + 20]
                nmb2 = sstat[:, base + 20:base + 22]
                for ti in range(2):
                    tt = 2 * pr + ti
                    par = tt % 2
                    bank = ["pb2a", "pb2b"][par]
                    pby = [pb2a, pb2b][par]
                    for oc in range(8):
                        P.add("tensor", lambda e, tt=tt, oc=oc, pby=pby: e.matmul(
                            pby[:, oc * 128:(oc + 1) * 128],
                            ybT[:, oc * 512 + tt * 128: oc * 512 + (tt + 1) * 128], sid[:],
                            start=True, stop=True),
                            waits=([("s", s_yb)] + bg(bank)) if oc == 0 else [],
                            inc="te" if oc == 7 else None)
                    tev = P.ctr["te"]
                    v_e2 = 0
                    for hf in range(2):
                        acol = sstat[:, base + ti * 2 + hf: base + ti * 2 + hf + 1]
                        v_e2 = P.add("vector", lambda e, tt=tt, hf=hf, par=par, pby=pby,
                                     acol=acol: e.scalar_tensor_tensor(
                            xft[par][:].rearrange("p (o u) -> p o u", o=8)[
                                :, hf * 4:(hf + 1) * 4, :],
                            pby[:, hf * 512:(hf + 1) * 512].rearrange("p (a u) -> p a u", a=4),
                            1.0,
                            x1n[:, tt * 1024:(tt + 1) * 1024].rearrange(
                                "p (o u) -> p o u", o=8)[:, hf * 4:(hf + 1) * 4, :],
                            ALU.mult, ALU.add, accum_out=acol),
                            waits=[("te", tev), ("s", s_prev_apply2[par])], inc="v")
                    bset(bank, "v", v_e2)
                    vdrain()
                    P.add("vector", lambda e, par=par, ti=ti, ssq2=ssq2: e.scalar_tensor_tensor(
                        x1nb[:], xft[par][:], 1.0, xft[par][:], ALU.mult, ALU.mult,
                        accum_out=ssq2[:, ti:ti + 1]),
                        waits=bg("x1nb"), inc="v")
                vdrain()
                P.add("vector", lambda e, s12=s12, h0s=h0s, h1s=h1s: e.tensor_tensor(
                    s12, h0s, h1s, ALU.add), inc="v")
                P.add("vector", lambda e, t12=t12, ssq2=ssq2: e.tensor_scalar(
                    t12, ssq2, D * 1e-5, 1.0 / D, ALU.add, ALU.mult), inc="v")
                vdrain()
                P.add("vector", lambda e, m2=m2, s12=s12: e.tensor_scalar(
                    m2, s12, 1.0 / D, None, ALU.mult), inc="v")
                vdrain()
                P.add("vector", lambda e, nm22=nm22, m2=m2: e.scalar_tensor_tensor(
                    nm22, m2, -1.0, m2, ALU.mult, ALU.mult), inc="v")
                vdrain()
                P.add("vector", lambda e, ve2=ve2, t12=t12, nm22=nm22: e.tensor_tensor(
                    ve2, t12, nm22, ALU.add), inc="v")
                vv = vdrain(inc="v")
                P.add("scalar", lambda e, std2=std2, ve2=ve2: e.activation(
                    std2, ve2, AF.Sqrt, bias=0.0, scale=1.0),
                    waits=[("v", vv)], inc="s")
                sq = sdrain(inc="s")
                P.add("vector", lambda e, rs2=rs2, std2=std2: e.reciprocal(rs2, std2),
                      waits=[("s", sq)], inc="v")
                vdrain()
                P.add("vector", lambda e, nmb2=nmb2, m2=m2, rs2=rs2: e.scalar_tensor_tensor(
                    nmb2, m2, -1.0, rs2, ALU.mult, ALU.mult), inc="v")
                vnm = vdrain(inc="v")
                for ti in range(2):
                    tt = 2 * pr + ti
                    par = tt % 2
                    rsc = rs2[:, ti:ti + 1]
                    nmbc = nmb2[:, ti:ti + 1]
                    gtile = c * NTT + tt
                    P.add("scalar", lambda e, par=par, rsc=rsc, nmbc=nmbc: e.activation(
                        x1t[par][:], xft[par][:], AF.Identity, bias=nmbc, scale=rsc),
                        waits=[("v", vnm), ("st%d" % par, x1t_st_guard[par])],
                        inc="s")
                    s_ap2 = sdrain(inc="s")
                    s_prev_apply2[par] = s_ap2
                    v_xft_guard[par] = ("s", s_ap2)
                    x1t_st_guard[par] = P.add("sync", lambda e, gtile=gtile, par=par: e.dma_start(
                        out=y_h.rearrange("(n p) d -> n p d", p=128)[gtile],
                        in_=x1t[par][:]),
                        waits=[("s", s_ap2)], inc="st%d" % par, n=16)

        P.add("sync", lambda e: e.nop(),
              waits=[("st0", P.ctr["st0"]), ("st1", P.ctr["st1"])])

        with nc.Block() as block:
            @block.sync
            def _(eng):
                P.replay("sync", eng)

            @block.gpsimd
            def _(eng):
                P.replay("gpsimd", eng)

            @block.tensor
            def _(eng):
                P.replay("tensor", eng)

            @block.vector
            def _(eng):
                P.replay("vector", eng)

            @block.scalar
            def _(eng):
                P.replay("scalar", eng)

    return nc


def host_inputs(L0_attn_W_qkv, L0_attn_b_qkv, L0_attn_W_out, L0_attn_b_out,
                L0_ffn_kernel, L0_ffn_input_scale, L0_ffn_kernel_scale, L0_ffn_bias):
    Wq = np.asarray(L0_attn_W_qkv)
    bq = np.asarray(L0_attn_b_qkv)
    Wo = np.asarray(L0_attn_W_out)
    bo = np.asarray(L0_attn_b_out)
    kern = np.asarray(L0_ffn_kernel)
    bff = np.asarray(L0_ffn_bias)

    perm = []
    for i in range(16):
        perm += list(range(i * 192, i * 192 + 64))
        perm += list(range(i * 192 + 64, i * 192 + 128))
    for i in range(8):
        perm += list(range((2 * i) * 192 + 128, (2 * i) * 192 + 192))
        perm += list(range((2 * i + 1) * 192 + 128, (2 * i + 1) * 192 + 192))
    perm = np.array(perm)

    wqkv = np.ascontiguousarray(Wq[:, perm].reshape(8, 128, 3072))
    bqkv = np.ascontiguousarray(bq[perm].reshape(24, 128).T)
    wout = np.ascontiguousarray(np.stack(
        [np.concatenate([Wo[64 * j:64 * j + 64], Wo[64 * (j + 8):64 * (j + 8) + 64]],
                        axis=0) for j in range(8)]))
    kern_r = np.ascontiguousarray(kern.reshape(8, 128, 1024))
    bout = np.ascontiguousarray(bo.reshape(8, 128).T)
    bffn = np.ascontiguousarray(bff.reshape(8, 128).T)
    sin = np.ascontiguousarray(np.asarray(L0_ffn_input_scale).reshape(1, 1).astype(np.float32))
    sk = np.ascontiguousarray(np.asarray(L0_ffn_kernel_scale).reshape(1, 1).astype(np.float32))

    ident = np.eye(128, dtype=ml_dtypes.bfloat16)
    p8c = np.zeros((8, 128), dtype=ml_dtypes.bfloat16)
    for cc in range(8):
        p8c[cc, np.arange(128) % 8 == cc] = 8.0
    ones1 = np.ones((1, 128), dtype=np.float32)

    return dict(wqkv=wqkv, bqkv=bqkv, wout=wout, kern=kern_r, bout=bout,
                bffn=bffn, sin=sin, sk=sk, ident=ident, p8c=p8c, ones1=ones1)


def kernel(L0_attn_x, L0_attn_W_qkv, L0_attn_b_qkv, L0_attn_W_out, L0_attn_b_out,
           L0_ffn_kernel, L0_ffn_input_scale, L0_ffn_kernel_scale, L0_ffn_bias):
    x = np.ascontiguousarray(np.asarray(L0_attn_x).reshape(-1, D))
    ntok = x.shape[0]
    tpc = ntok // NCORE
    shared = host_inputs(L0_attn_W_qkv, L0_attn_b_qkv, L0_attn_W_out, L0_attn_b_out,
                         L0_ffn_kernel, L0_ffn_input_scale, L0_ffn_kernel_scale,
                         L0_ffn_bias)
    nc = build_program(tpc)
    in_maps = []
    for i in range(NCORE):
        m = dict(shared)
        m["x"] = np.ascontiguousarray(x[i * tpc:(i + 1) * tpc])
        in_maps.append(m)
    res = run_bass_kernel_spmd(nc, in_maps, core_ids=list(range(NCORE)))
    out = np.concatenate([np.asarray(r["y"]) for r in res.results], axis=0)
    return out.reshape(np.asarray(L0_attn_x).shape).astype(np.float32)


# version 16
# speedup vs baseline: 1.0292x; 1.0292x over previous
import contextlib
import numpy as np
import ml_dtypes
import concourse.bass as bass
import concourse.mybir as mybir
from concourse.bass_utils import run_bass_kernel_spmd

F32 = mybir.dt.float32
BF16 = mybir.dt.bfloat16
FP8 = mybir.dt.float8e4
AX = mybir.AxisListType
AF = mybir.ActivationFunctionType
ALU = mybir.AluOpType

H = 16
DK = 64
D = 1024
TC = 512
NTT = TC // 128
NB = TC // 64
NCORE = 8

RIN, RSK, SPROD = 0, 1, 2

OWN = {"tensor": "te", "vector": "v", "scalar": "s", "sync": None, "gpsimd": None}


class Prog:

    def __init__(self, nc, stack):
        self.nc = nc
        self.items = {e: [] for e in OWN}
        self.ctr = {}
        self.sems = {}
        for name in ("te", "v", "s", "ld", "lf0", "lf1", "gw", "gk",
                     "gb0", "gb1", "gb2", "gb3", "gr", "gd", "st0", "st1"):
            self.sems[name] = stack.enter_context(nc.semaphore(name))
            self.ctr[name] = 0

    def add(self, eng, fn, waits=(), inc=None, n=1, selfw=False):
        w = [(s, v) for (s, v) in waits if v > 0]
        if inc is None and OWN[eng] in ("v", "s"):
            inc = OWN[eng]
        self.items[eng].append((w, fn, inc, n, selfw))
        if inc is not None:
            self.ctr[inc] += n
            return self.ctr[inc]
        return None

    def replay(self, engname, engobj):
        waited = {}
        own = OWN[engname]
        for waits, fn, inc, n, selfw in self.items[engname]:
            for sname, val in waits:
                if sname == own and not selfw:
                    continue
                if waited.get(sname, 0) >= val:
                    continue
                engobj.wait_ge(self.sems[sname], val)
                waited[sname] = val
            ins = fn(engobj)
            if inc is not None:
                ins.then_inc(self.sems[inc], n)


def build_program(T, dbg=False):
    NCH = T // TC
    nc = bass.Bass("TRN2", target_bir_lowering=False, debug=False)

    x_h = nc.dram_tensor("x", [T, D], F32, kind="ExternalInput").ap()
    wqkv_h = nc.dram_tensor("wqkv", [8, 128, 3072], F32, kind="ExternalInput").ap()
    wout_h = nc.dram_tensor("wout", [8, 128, 1024], F32, kind="ExternalInput").ap()
    kern_h = nc.dram_tensor("kern", [8, 128, 1024], BF16, kind="ExternalInput").ap()
    bqkv_h = nc.dram_tensor("bqkv", [128, 24], F32, kind="ExternalInput").ap()
    bout_h = nc.dram_tensor("bout", [128, 8], F32, kind="ExternalInput").ap()
    bffn_h = nc.dram_tensor("bffn", [128, 8], BF16, kind="ExternalInput").ap()
    sin_h = nc.dram_tensor("sin", [1, 1], F32, kind="ExternalInput").ap()
    sk_h = nc.dram_tensor("sk", [1, 1], F32, kind="ExternalInput").ap()
    ident_h = nc.dram_tensor("ident", [128, 128], BF16, kind="ExternalInput").ap()
    p8_h = nc.dram_tensor("p8c", [8, 128], BF16, kind="ExternalInput").ap()
    ones_h = nc.dram_tensor("ones1", [1, 128], F32, kind="ExternalInput").ap()
    y_h = nc.dram_tensor("y", [T, D], F32, kind="ExternalOutput").ap()

    dbg_h = {}
    if dbg:
        for name, shape, dt in [
            ("d_q", [64, 8192], BF16), ("d_k", [128, 8192], BF16),
            ("d_v", [128, 8192], BF16), ("d_e", [128, 1024], BF16),
            ("d_den", [128, 8], F32), ("d_at", [128, 1024], BF16),
            ("d_vt", [128, 1024], BF16), ("d_ctx", [128, 4096], BF16),
            ("d_x1", [128, 4096], F32), ("d_x1n", [128, 4096], F32),
            ("d_xq", [128, 4096], FP8), ("d_yb", [128, 4096], BF16),
            ("d_an", [128, 1024], BF16), ("d_stat", [128, 88], F32),
            ("d_xt", [128, 4096], BF16),
        ]:
            dbg_h[name] = nc.dram_tensor(name, shape, dt, kind="ExternalOutput").ap()

    st = contextlib.ExitStack()
    en = st.enter_context
    with st:
        swq = en(nc.sbuf_tensor("swq", [128, 24576], BF16))
        swo = en(nc.sbuf_tensor("swo", [128, 8192], BF16))
        skq = en(nc.sbuf_tensor("skq", [128, 8192], FP8))
        sbq = en(nc.sbuf_tensor("sbq", [128, 24], F32))
        sbo = en(nc.sbuf_tensor("sbo", [128, 8], F32))
        sbfb = en(nc.sbuf_tensor("sbfb", [128, 8], BF16))
        sbf32 = en(nc.sbuf_tensor("sbf32", [128, 8], F32))
        sid = en(nc.sbuf_tensor("sid", [128, 128], BF16))
        sp8 = en(nc.sbuf_tensor("sp8", [8, 128], BF16))
        sones = en(nc.sbuf_tensor("sones", [1, 128], F32))
        ssc = en(nc.sbuf_tensor("ssc", [1, 4], F32))
        ssc2 = en(nc.sbuf_tensor("ssc2", [1, 4], F32))
        srcp = en(nc.sbuf_tensor("srcp", [128, 4], F32))
        szero = en(nc.sbuf_tensor("szero", [128, 1], F32))
        negc = en(nc.sbuf_tensor("negc", [128, 1], F32))
        qbuf = en(nc.sbuf_tensor("qbuf", [64, 8192], BF16))
        kbuf = en(nc.sbuf_tensor("kbuf", [128, 8192], BF16))
        vbuf = en(nc.sbuf_tensor("vbuf", [128, 8192], BF16))
        xT = en(nc.sbuf_tensor("xT", [128, 4096], BF16))
        xb = [en(nc.sbuf_tensor(f"xb{i}", [128, 1024], BF16)) for i in range(4)]
        xft = [en(nc.sbuf_tensor(f"xft{i}", [128, 1024], F32)) for i in range(2)]
        x1t = [en(nc.sbuf_tensor(f"x1t{i}", [128, 1024], F32)) for i in range(2)]
        sE = en(nc.sbuf_tensor("sE", [128, 1024], BF16))
        sAttn = en(nc.sbuf_tensor("sAttn", [128, 1024], BF16))
        sAT = en(nc.sbuf_tensor("sAT", [128, 1024], BF16))
        sVT = en(nc.sbuf_tensor("sVT", [128, 1024], BF16))
        sden = en(nc.sbuf_tensor("sden", [128, 8], F32))
        sdrc = en(nc.sbuf_tensor("sdrc", [128, 8], F32))
        ctx = en(nc.sbuf_tensor("ctx", [128, 4096], BF16))
        aoTc = en(nc.sbuf_tensor("aoTc", [128, 4096], BF16))
        x1n = en(nc.sbuf_tensor("x1n", [128, 4096], F32))
        x1nb = en(nc.sbuf_tensor("x1nb", [128, 1024], BF16))
        xqT = en(nc.sbuf_tensor("xqT", [128, 4096], FP8))
        ybT = en(nc.sbuf_tensor("ybT", [128, 4096], BF16))
        sstat = en(nc.sbuf_tensor("sstat", [128, 88], F32))

        pb2a = en(nc.psum_tensor("pb2a", [128, 1024], F32))
        pb2b = en(nc.psum_tensor("pb2b", [128, 1024], F32))
        pb2c = en(nc.psum_tensor("pb2c", [128, 1024], F32))
        pb1a = en(nc.psum_tensor("pb1a", [128, 512], F32))
        pb1b = en(nc.psum_tensor("pb1b", [128, 512], F32))

        P = Prog(nc, st)

        bguard = {}

        def bset(name, sem, val):
            bguard[name] = [(sem, val)]

        def badd(name, sem, val):
            bguard.setdefault(name, []).append((sem, val))

        def bg(*names):
            out = []
            for nm in names:
                out += bguard.get(nm, [])
            return out

        def vdrain(inc=None):
            return P.add("vector", lambda e: e.drain(),
                         waits=[("v", P.ctr["v"])], selfw=True, inc=inc)

        def sdrain(inc=None):
            return P.add("scalar", lambda e: e.drain(),
                         waits=[("s", P.ctr["s"])], selfw=True, inc=inc)

        P.add("sync", lambda e: e.dma_start(out=sid[:], in_=ident_h[:]), inc="ld", n=16)
        P.add("sync", lambda e: e.dma_start(out=sp8[:], in_=p8_h[:]), inc="ld", n=16)
        P.add("sync", lambda e: e.dma_start(out=sones[:], in_=ones_h[:]), inc="ld", n=16)
        P.add("sync", lambda e: e.dma_start(out=sbq[:], in_=bqkv_h[:]), inc="ld", n=16)
        P.add("sync", lambda e: e.dma_start(out=sbo[:], in_=bout_h[:]), inc="ld", n=16)
        P.add("sync", lambda e: e.dma_start(out=sbfb[:], in_=bffn_h[:]), inc="ld", n=16)
        P.add("sync", lambda e: e.dma_start(out=ssc[0:1, 0:1], in_=sin_h[:]), inc="ld", n=16)
        ld_consts = P.add("sync", lambda e: e.dma_start(out=ssc[0:1, 1:2], in_=sk_h[:]),
                          inc="ld", n=16)

        P.add("gpsimd", lambda e: e.dma_start(
            out=swq[:].rearrange("p (c f) -> p c f", c=8), in_=wqkv_h.rearrange("c p f -> p c f")), inc="gw", n=16)
        gp_w = P.add("gpsimd", lambda e: e.dma_start(
            out=swo[:].rearrange("p (j f) -> p j f", j=8), in_=wout_h.rearrange("j p f -> p j f")), inc="gw", n=16)

        P.add("vector", lambda e: e.tensor_copy(sbf32[:], sbfb[:]),
              waits=[("ld", ld_consts)], inc="v")
        P.add("vector", lambda e: e.reciprocal(ssc2[0:1, 0:2], ssc[0:1, 0:2]), inc="v")
        P.add("vector", lambda e: e.tensor_mul(
            ssc2[0:1, 2:3], ssc[0:1, 0:1], ssc[0:1, 1:2]), inc="v")
        v_rc = vdrain(inc="v")
        for k in range(3):
            P.add("tensor", lambda e, k=k: e.matmul(
                pb1a[:, k:k + 1], sones[0:1, :], ssc2[0:1, k:k + 1], start=True, stop=True),
                waits=[("v", v_rc)], inc="te" if k == 2 else None)
        te_bc = P.ctr["te"]
        P.add("vector", lambda e: e.tensor_copy(srcp[:, 0:3], pb1a[:, 0:3]),
              waits=[("te", te_bc)], inc="v")
        v_srcp = vdrain(inc="v")
        bset("pb1a", "v", v_srcp)
        P.add("vector", lambda e: e.memset(szero[:], 0.0), inc="v")
        P.add("vector", lambda e: e.memset(vbuf[:], 0.0), inc="v")
        v_vz = vdrain(inc="v")

        s_q = 0
        for cc in range(8):
            ldv = P.add("gpsimd", lambda e, cc=cc: e.dma_start(out=x1nb[:], in_=kern_h[cc]),
                        waits=[("s", s_q)], inc="gk", n=16)
            s_q = P.add("scalar", lambda e, cc=cc: e.activation(
                skq[:, cc * 1024:(cc + 1) * 1024], x1nb[:], AF.Identity, bias=0.0,
                scale=srcp[:, RSK:RSK + 1]),
                waits=[("gk", ldv), ("v", v_srcp)], inc="s")
        bset("x1nb", "s", s_q)

        def q_dest(h):
            return qbuf[0:64, :].rearrange("p (G u) -> p G u", G=64)[:, :, h * 8:(h + 1) * 8]

        def v_dest(g):
            rh = g % 2
            return vbuf[rh * 64:(rh + 1) * 64, :].rearrange(
                "p (G u) -> p G u", G=64)[:, :, g * 8:(g + 1) * 8]

        v_negc = 0
        s_prev_apply1 = [0, 0]
        s_prev_apply2 = [0, 0]
        v_xft_guard = [("v", 0), ("v", 0)]
        x1t_st_guard = [0, 0]
        st_out = 0

        for c in range(NCH):
            v_xT_done = 0
            for tt in range(NTT):
                ldb = P.add("gpsimd", lambda e, c=c, tt=tt: e.dma_start(
                    out=xb[tt % 4][:],
                    in_=x_h.rearrange("(n p) d -> n p d", p=128)[c * NTT + tt]),
                    waits=bg(("xb", tt % 4)), inc="gb%d" % (tt % 4), n=16)
                bank = ["pb2a", "pb2b"][tt % 2]
                pbx = [pb2a, pb2b][tt % 2]
                for ec in range(8):
                    P.add("tensor", lambda e, tt=tt, ec=ec, pbx=pbx: e.matmul(
                        pbx[:, ec * 128:(ec + 1) * 128],
                        xb[tt % 4][:, ec * 128:(ec + 1) * 128], sid[:],
                        start=True, stop=True),
                        waits=([("gb%d" % (tt % 4), ldb), ("ld", ld_consts)] + bg(bank))
                        if ec == 0 else [],
                        inc="te" if ec == 7 else None)
                tev = P.ctr["te"]
                bset(("xb", tt % 4), "te", tev)
                for hf in range(2):
                    v_xT_done = P.add("vector", lambda e, tt=tt, hf=hf, pbx=pbx: e.tensor_copy(
                        xT[:].rearrange("p (ec u) -> p ec u", ec=8)[
                            :, hf * 4:(hf + 1) * 4, tt * 128:(tt + 1) * 128],
                        pbx[:, hf * 512:(hf + 1) * 512].rearrange("p (a u) -> p a u", a=4)),
                        waits=[("te", tev)], inc="v")
                bset(bank, "v", v_xT_done)

            v_evac = 0
            s_evac = 0
            for oc in range(24):
                bank = ["pb1a", "pb1b"][oc % 2]
                pbq = [pb1a, pb1b][oc % 2]
                for kc in range(8):
                    P.add("tensor", lambda e, oc=oc, kc=kc, pbq=pbq: e.matmul(
                        pbq[:], swq[:, kc * 3072 + oc * 128: kc * 3072 + (oc + 1) * 128],
                        xT[:, kc * 512:(kc + 1) * 512],
                        start=(kc == 0), stop=(kc == 7)),
                        waits=([("v", v_xT_done), ("gw", gp_w)] + bg(bank))
                        if kc == 0 else [],
                        inc="te" if kc == 7 else None)
                tev = P.ctr["te"]
                guards = []
                if oc < 16:
                    hq = oc
                    v_evac = P.add("vector", lambda e, oc=oc, hq=hq, pbq=pbq: e.tensor_scalar(
                        q_dest(hq),
                        pbq[0:64, :].rearrange("p (G t) -> p G t", G=64),
                        sbq[0:64, oc:oc + 1], 0.125, ALU.add, ALU.mult),
                        waits=[("te", tev)], inc="v")
                    guards.append(("v", v_evac))
                    gk = oc
                    if oc < 8:
                        v_evac = P.add("vector", lambda e, oc=oc, gk=gk, pbq=pbq: e.tensor_scalar(
                            kbuf[64:128, gk * 512:(gk + 1) * 512], pbq[64:128, :],
                            sbq[64:128, oc:oc + 1], None, ALU.add),
                            waits=[("te", tev)], inc="v")
                        guards.append(("v", v_evac))
                    else:
                        s_evac = P.add("scalar", lambda e, oc=oc, gk=gk, pbq=pbq: e.activation(
                            kbuf[64:128, gk * 512:(gk + 1) * 512], pbq[64:128, :],
                            AF.Identity, bias=sbq[64:128, oc:oc + 1], scale=1.0),
                            waits=[("te", tev)], inc="s")
                        guards.append(("s", s_evac))
                else:
                    i = oc - 16
                    for hf, g in ((0, 2 * i), (1, 2 * i + 1)):
                        s_evac = P.add("scalar", lambda e, hf=hf, g=g, oc=oc, pbq=pbq: e.activation(
                            v_dest(g),
                            pbq[hf * 64:(hf + 1) * 64, :].rearrange("p (G t) -> p G t", G=64),
                            AF.Identity, bias=sbq[hf * 64:(hf + 1) * 64, oc:oc + 1], scale=1.0),
                            waits=[("te", tev), ("v", v_vz)], inc="s")
                        guards.append(("s", s_evac))
                bguard[bank] = guards
            v_qkv_evac = vdrain(inc="v")
            s_qkv_evac = sdrain(inc="s")
            gp_rep = P.add("gpsimd", lambda e: e.dma_start(
                out=kbuf[0:64, :], in_=kbuf[64:128, :]),
                waits=[("v", v_qkv_evac), ("s", s_qkv_evac)], inc="gr", n=16)

            if dbg and c == 0:
                P.add("gpsimd", lambda e: e.dma_start(out=dbg_h["d_xt"][:], in_=xT[:]),
                      waits=[("v", v_xT_done)], inc="gd", n=16)
                P.add("gpsimd", lambda e: e.dma_start(out=dbg_h["d_q"][:], in_=qbuf[:]),
                      waits=[("v", v_qkv_evac), ("s", s_qkv_evac)], inc="gd", n=16)
                P.add("gpsimd", lambda e: e.dma_start(out=dbg_h["d_k"][:], in_=kbuf[:]),
                      waits=[("gr", gp_rep)], inc="gd", n=16)
                P.add("gpsimd", lambda e: e.dma_start(out=dbg_h["d_v"][:], in_=vbuf[:]),
                      waits=[("v", v_qkv_evac), ("s", s_qkv_evac)], inc="gd", n=16)

            s_exp = 0
            v_attn = 0
            s_attn = 0
            te_at = 0
            te_mm3 = 0
            s_ctx_evac = 0
            v_dbg_den = 0
            gp_dbg_atvt = 0
            for b in range(NB):
                v_dbg_den = v_dbg_den if b else 0
                for gl in range(8):
                    G = b * 8 + gl
                    P.add("tensor", lambda e, gl=gl: e.matmul(
                        pb2a[:, gl * 128:(gl + 1) * 128], sp8[0:8, :], sp8[0:8, :],
                        start=True, stop=False),
                        waits=([("v", v_qkv_evac), ("s", s_qkv_evac), ("gr", gp_rep)]
                               + bg("pb2a")) if gl == 0 else [])
                    P.add("tensor", lambda e, G=G, gl=gl: e.matmul(
                        pb2a[:, gl * 128:(gl + 1) * 128],
                        qbuf[:, G * 128:(G + 1) * 128],
                        kbuf[0:64, :].rearrange("p (g u) -> p g u", g=16)[
                            :, :, G * 8:(G + 1) * 8],
                        start=False, stop=True),
                        inc="te" if gl == 7 else None)
                te_mm1 = P.ctr["te"]
                if c == 0 and b == 0:
                    P.add("vector", lambda e: e.reduce_max(
                        negc[:, 0:1], pb2a[:], axis=AX.X, negate=True),
                        waits=[("te", te_mm1)], inc="v")
                    v_negc = vdrain(inc="v")
                s_exp = P.add("scalar", lambda e: e.activation(
                    sE[:], pb2a[:], AF.Exp, bias=negc[:, 0:1], scale=1.0),
                    waits=[("te", te_mm1), ("v", v_negc), ("v", v_attn)],
                    inc="s")
                bset("pb2a", "s", s_exp)
                gp_dbg = 0
                if dbg and c == 0 and b == 0:
                    gp_dbg = P.add("gpsimd", lambda e: e.dma_start(out=dbg_h["d_e"][:], in_=sE[:]),
                                   waits=[("s", s_exp)], inc="gd", n=16)
                P.add("vector", lambda e: e.reduce_sum(
                    sden[:, 0:8], sE[:].rearrange("p (g u) -> p g u", g=8), axis=AX.X),
                    waits=[("s", s_exp), ("gd", v_dbg_den), ("gd", gp_dbg)], inc="v")
                if dbg and c == 0 and b == 0:
                    vden = P.ctr["v"]
                    gp_dbg = P.add("gpsimd", lambda e: e.dma_start(out=dbg_h["d_den"][:], in_=sden[:]),
                                   waits=[("v", vden)], inc="gd", n=16)
                    v_dbg_den_next = gp_dbg
                else:
                    v_dbg_den_next = 0
                v_dbg_den = v_dbg_den_next
                P.add("vector", lambda e: e.drain())
                P.add("vector", lambda e: e.reciprocal(sdrc[:, 0:8], sden[:, 0:8]),
                      waits=[("s", s_attn)], inc="v")
                v_rcp = vdrain(inc="v")
                v_attn = P.add("vector", lambda e: e.tensor_tensor(
                    sAttn[:, 0:640].rearrange("p (g u) -> p g u", g=5),
                    sE[:, 0:640].rearrange("p (g u) -> p g u", g=5),
                    sdrc[:, 0:5].rearrange("p (g o) -> p g o", o=1).broadcast_to([128, 5, 128]),
                    ALU.mult), waits=[("gd", gp_dbg), ("te", te_at)], inc="v")
                s_attn = 0
                for g5 in range(5, 8):
                    s_attn = P.add("scalar", lambda e, g5=g5: e.activation(
                        sAttn[:, g5 * 128:(g5 + 1) * 128],
                        sE[:, g5 * 128:(g5 + 1) * 128], AF.Identity,
                        bias=0.0, scale=sdrc[:, g5:g5 + 1]),
                        waits=[("v", v_rcp), ("te", te_at)] if g5 == 5 else [],
                        inc="s")
                gp_dbg_an = 0
                if dbg and c == 0 and b == 0:
                    gp_dbg_an = P.add("gpsimd", lambda e: e.dma_start(
                        out=dbg_h["d_an"][:], in_=sAttn[:]),
                        waits=[("v", v_attn), ("s", s_attn)], inc="gd", n=16)
                for gl in range(8):
                    G = b * 8 + gl
                    P.add("tensor", lambda e, G=G, gl=gl: e.matmul(
                        pb2c[:, gl * 128:(gl + 1) * 128],
                        vbuf[:, G * 128:(G + 1) * 128], sid[:],
                        start=True, stop=True),
                        waits=([("v", v_qkv_evac), ("s", s_qkv_evac)] + bg("pb2c"))
                        if gl == 0 else [],
                        inc="te" if gl == 7 else None)
                te_vt = P.ctr["te"]
                v_vt = P.add("vector", lambda e: e.tensor_tensor(
                    sVT[:, 0:512], pb2c[:, 0:512],
                    szero[:, 0:1].broadcast_to([128, 512]), ALU.add),
                    waits=[("te", te_vt), ("te", te_mm3), ("gd", gp_dbg_atvt)], inc="v")
                s_vt = P.add("scalar", lambda e: e.copy(
                    sVT[:, 512:1024], pb2c[:, 512:1024]),
                    waits=[("te", te_vt), ("te", te_mm3), ("gd", gp_dbg_atvt)], inc="s")
                bguard["pb2c"] = [("v", v_vt), ("s", s_vt)]
                for gl in range(8):
                    P.add("tensor", lambda e, gl=gl: e.matmul(
                        pb2b[:, gl * 128:(gl + 1) * 128],
                        sAttn[:, gl * 128:(gl + 1) * 128], sid[:],
                        start=True, stop=True),
                        waits=([("v", v_attn), ("gd", gp_dbg_an)] + bg("pb2b"))
                        if gl == 0 else ([("s", s_attn)] if gl == 5 else []),
                        inc="te" if gl == 7 else None)
                te_at = P.ctr["te"]
                v_at = P.add("vector", lambda e: e.tensor_tensor(
                    sAT[:, 0:512], pb2b[:, 0:512],
                    szero[:, 0:1].broadcast_to([128, 512]), ALU.add),
                    waits=[("te", te_at), ("te", te_mm3), ("gd", gp_dbg_atvt)], inc="v")
                s_at = P.add("scalar", lambda e: e.copy(sAT[:, 512:1024], pb2b[:, 512:1024]),
                             waits=[("te", te_at), ("te", te_mm3), ("gd", gp_dbg_atvt)],
                             inc="s")
                bguard["pb2b"] = [("v", v_at), ("s", s_at)]
                if dbg and c == 0 and b == 0:
                    P.add("gpsimd", lambda e: e.dma_start(out=dbg_h["d_at"][:], in_=sAT[:]),
                          waits=[("v", v_at)], inc="gd", n=16)
                    gp_dbg_atvt = P.add("gpsimd", lambda e: e.dma_start(
                        out=dbg_h["d_vt"][:], in_=sVT[:]),
                        waits=[("s", s_vt)], inc="gd", n=16)
                for gl in range(8):
                    for bb in range(2):
                        for pp in range(2):
                            P.add("tensor", lambda e, gl=gl, bb=bb, pp=pp: e.matmul(
                                pb1a[bb * 64:(bb + 1) * 64, gl * 64:(gl + 1) * 64],
                                sVT[:, gl * 128 + pp * 64: gl * 128 + (pp + 1) * 64],
                                sAT[:, gl * 128 + bb * 64: gl * 128 + (bb + 1) * 64],
                                start=(pp == 0), stop=(pp == 1),
                                tile_position=(0, bb * 64)),
                                waits=([("v", v_at), ("s", s_at), ("v", v_vt),
                                        ("s", s_vt)] + bg("pb1a"))
                                if (gl == 0 and bb == 0 and pp == 0) else [],
                                inc="te" if (gl == 7 and bb == 1 and pp == 1) else None)
                te_mm3 = P.ctr["te"]
                if b % 2 == 0:
                    v_ctx = P.add("vector", lambda e, b=b: e.tensor_tensor(
                        ctx[:, b * 512:(b + 1) * 512], pb1a[:],
                        szero[:, 0:1].broadcast_to([128, 512]), ALU.add),
                        waits=[("te", te_mm3)], inc="v")
                    bguard["pb1a"] = [("v", v_ctx)]
                else:
                    s_ctx_evac = P.add("scalar", lambda e, b=b: e.copy(
                        ctx[:, b * 512:(b + 1) * 512], pb1a[:]),
                        waits=[("te", te_mm3)], inc="s")
                    bguard["pb1a"] = [("s", s_ctx_evac)]
            if dbg and c == 0:
                P.add("gpsimd", lambda e: e.dma_start(out=dbg_h["d_ctx"][:], in_=ctx[:]),
                      waits=[("s", s_ctx_evac)], inc="gd", n=16)

            s_ao = 0
            for oc in range(8):
                bank = ["pb1b", "pb1a"][oc % 2]
                pbw = [pb1b, pb1a][oc % 2]
                for j in range(8):
                    P.add("tensor", lambda e, oc=oc, j=j, pbw=pbw: e.matmul(
                        pbw[:], swo[:, j * 1024 + oc * 128: j * 1024 + (oc + 1) * 128],
                        ctx[:].rearrange("p (G u) -> p G u", G=64)[:, :, j * 8:(j + 1) * 8],
                        start=(j == 0), stop=(j == 7)),
                        waits=([("s", s_ctx_evac), ("v", v_ctx)] + bg(bank))
                        if j == 0 else [],
                        inc="te" if j == 7 else None)
                tev = P.ctr["te"]
                s_ao = P.add("scalar", lambda e, oc=oc, pbw=pbw: e.activation(
                    aoTc[:, oc * 512:(oc + 1) * 512], pbw[:], AF.Identity,
                    bias=sbo[:, oc:oc + 1], scale=1.0),
                    waits=[("te", tev)], inc="s")
                bset(bank, "s", s_ao)

            s_quant = 0
            for pr in range(2):
                base = pr * 22
                h0s = sstat[:, base:base + 4].rearrange("p (t h) -> p t h", t=2)[:, :, 0:1]
                h1s = sstat[:, base:base + 4].rearrange("p (t h) -> p t h", t=2)[:, :, 1:2]
                ssq2 = sstat[:, base + 4:base + 6]
                s12 = sstat[:, base + 6:base + 8]
                m2 = sstat[:, base + 8:base + 10]
                t12 = sstat[:, base + 10:base + 12]
                nm22 = sstat[:, base + 12:base + 14]
                ve2 = sstat[:, base + 14:base + 16]
                std2 = sstat[:, base + 16:base + 18]
                rs2 = sstat[:, base + 18:base + 20]
                nmb2 = sstat[:, base + 20:base + 22]
                for ti in range(2):
                    tt = 2 * pr + ti
                    par = tt % 2
                    ldf = P.add("sync", lambda e, c=c, tt=tt, par=par: e.dma_start(
                        out=xft[par][:],
                        in_=x_h.rearrange("(n p) d -> n p d", p=128)[c * NTT + tt]),
                        waits=[v_xft_guard[par]], inc="lf%d" % par, n=16)
                    for oc in range(8):
                        P.add("tensor", lambda e, tt=tt, oc=oc: e.matmul(
                            pb2a[:, oc * 128:(oc + 1) * 128],
                            aoTc[:, oc * 512 + tt * 128: oc * 512 + (tt + 1) * 128], sid[:],
                            start=True, stop=True),
                            waits=([("s", s_ao)] + bg("pb2a")) if oc == 0 else [],
                            inc="te" if oc == 7 else None)
                    tev = P.ctr["te"]
                    v_ev2 = 0
                    for hf in range(2):
                        acol = sstat[:, base + ti * 2 + hf: base + ti * 2 + hf + 1]
                        v_ev2 = P.add("vector", lambda e, hf=hf, par=par, acol=acol: e.scalar_tensor_tensor(
                            x1t[par][:].rearrange("p (o u) -> p o u", o=8)[
                                :, hf * 4:(hf + 1) * 4, :],
                            pb2a[:, hf * 512:(hf + 1) * 512].rearrange("p (a u) -> p a u", a=4),
                            1.0,
                            xft[par][:].rearrange("p (o u) -> p o u", o=8)[
                                :, hf * 4:(hf + 1) * 4, :],
                            ALU.mult, ALU.add, accum_out=acol),
                            waits=[("te", tev), ("lf%d" % par, ldf),
                                   ("s", s_prev_apply1[par]),
                                   ("st%d" % par, x1t_st_guard[par])],
                            inc="v")
                    bset("pb2a", "v", v_ev2)
                    v_xft_guard[par] = ("v", v_ev2)
                    vdrain()
                    P.add("vector", lambda e, par=par, ti=ti, ssq2=ssq2: e.scalar_tensor_tensor(
                        x1nb[:], x1t[par][:], 1.0, x1t[par][:], ALU.mult, ALU.mult,
                        accum_out=ssq2[:, ti:ti + 1]),
                        waits=bg("x1nb"), inc="v")
                vdrain()
                P.add("vector", lambda e, s12=s12, h0s=h0s, h1s=h1s: e.tensor_tensor(
                    s12, h0s, h1s, ALU.add), inc="v")
                P.add("vector", lambda e, t12=t12, ssq2=ssq2: e.tensor_scalar(
                    t12, ssq2, D * 1e-5, 1.0 / D, ALU.add, ALU.mult), inc="v")
                vdrain()
                P.add("vector", lambda e, m2=m2, s12=s12: e.tensor_scalar(
                    m2, s12, 1.0 / D, None, ALU.mult), inc="v")
                vdrain()
                P.add("vector", lambda e, nm22=nm22, m2=m2: e.scalar_tensor_tensor(
                    nm22, m2, -1.0, m2, ALU.mult, ALU.mult), inc="v")
                vdrain()
                P.add("vector", lambda e, ve2=ve2, t12=t12, nm22=nm22: e.tensor_tensor(
                    ve2, t12, nm22, ALU.add), inc="v")
                vv = vdrain(inc="v")
                P.add("scalar", lambda e, std2=std2, ve2=ve2: e.activation(
                    std2, ve2, AF.Sqrt, bias=0.0, scale=1.0),
                    waits=[("v", vv)], inc="s")
                sq = sdrain(inc="s")
                P.add("vector", lambda e, rs2=rs2, std2=std2: e.reciprocal(rs2, std2),
                      waits=[("s", sq)], inc="v")
                vdrain()
                P.add("vector", lambda e, nmb2=nmb2, m2=m2, rs2=rs2: e.scalar_tensor_tensor(
                    nmb2, m2, -1.0, rs2, ALU.mult, ALU.mult), inc="v")
                vnm = vdrain(inc="v")
                for ti in range(2):
                    tt = 2 * pr + ti
                    par = tt % 2
                    rsc = rs2[:, ti:ti + 1]
                    nmbc = nmb2[:, ti:ti + 1]
                    s_ap = P.add("scalar", lambda e, tt=tt, par=par, rsc=rsc, nmbc=nmbc: e.activation(
                        x1n[:, tt * 1024:(tt + 1) * 1024], x1t[par][:],
                        AF.Identity, bias=nmbc, scale=rsc),
                        waits=[("v", vnm)], inc="s")
                    s_prev_apply1[par] = s_ap
                    vcb = P.add("scalar", lambda e, tt=tt: e.activation(
                        x1nb[:], x1n[:, tt * 1024:(tt + 1) * 1024],
                        AF.Identity, bias=0.0, scale=1.0),
                        waits=bg("x1nb"), inc="s")
                    for ec in range(8):
                        P.add("tensor", lambda e, ec=ec: e.matmul(
                            pb2b[:, ec * 128:(ec + 1) * 128],
                            x1nb[:, ec * 128:(ec + 1) * 128], sid[:],
                            start=True, stop=True),
                            waits=([("s", vcb)] + bg("pb2b")) if ec == 0 else [],
                            inc="te" if ec == 7 else None)
                    tevq = P.ctr["te"]
                    bset("x1nb", "te", tevq)
                    for hf in range(2):
                        s_quant = P.add("scalar", lambda e, tt=tt, hf=hf: e.activation(
                            xqT[:].rearrange("p (ec u) -> p ec u", ec=8)[
                                :, hf * 4:(hf + 1) * 4, tt * 128:(tt + 1) * 128],
                            pb2b[:, hf * 512:(hf + 1) * 512].rearrange("p (a u) -> p a u", a=4),
                            AF.Identity, bias=0.0, scale=srcp[:, RIN:RIN + 1]),
                            waits=[("te", tevq)], inc="s")
                    bset("pb2b", "s", s_quant)
            if dbg and c == 0:
                P.add("gpsimd", lambda e: e.dma_start(out=dbg_h["d_stat"][:], in_=sstat[:]),
                      waits=[("s", P.ctr["s"]), ("v", P.ctr["v"])], inc="gd", n=16)
                P.add("gpsimd", lambda e: e.dma_start(out=dbg_h["d_x1n"][:], in_=x1n[:]),
                      waits=[("s", P.ctr["s"])], inc="gd", n=16)
                P.add("gpsimd", lambda e: e.dma_start(out=dbg_h["d_xq"][:], in_=xqT[:]),
                      waits=[("s", s_quant)], inc="gd", n=16)
                P.add("gpsimd", lambda e: e.dma_start(out=dbg_h["d_x1"][:], in_=x1n[:]),
                      waits=[("s", P.ctr["s"])], inc="gd", n=16)

            s_yb = 0
            for oc in range(8):
                bank = ["pb1a", "pb1b"][oc % 2]
                pbf = [pb1a, pb1b][oc % 2]
                for ec in range(8):
                    P.add("tensor", lambda e, oc=oc, ec=ec, pbf=pbf: e.matmul(
                        pbf[:], skq[:, ec * 1024 + oc * 128: ec * 1024 + (oc + 1) * 128],
                        xqT[:, ec * 512:(ec + 1) * 512],
                        start=(ec == 0), stop=(ec == 7)),
                        waits=([("s", s_quant)] + bg(bank)) if ec == 0 else [],
                        inc="te" if ec == 7 else None)
                tev = P.ctr["te"]
                s_yb = P.add("scalar", lambda e, oc=oc, pbf=pbf: e.activation(
                    ybT[:, oc * 512:(oc + 1) * 512], pbf[:], AF.Identity,
                    bias=sbf32[:, oc:oc + 1], scale=srcp[:, SPROD:SPROD + 1]),
                    waits=[("te", tev)], inc="s")
                bset(bank, "s", s_yb)
            if dbg and c == 0:
                P.add("gpsimd", lambda e: e.dma_start(out=dbg_h["d_yb"][:], in_=ybT[:]),
                      waits=[("s", s_yb)], inc="gd", n=16)

            for pr in range(2):
                base = 44 + pr * 22
                h0s = sstat[:, base:base + 4].rearrange("p (t h) -> p t h", t=2)[:, :, 0:1]
                h1s = sstat[:, base:base + 4].rearrange("p (t h) -> p t h", t=2)[:, :, 1:2]
                ssq2 = sstat[:, base + 4:base + 6]
                s12 = sstat[:, base + 6:base + 8]
                m2 = sstat[:, base + 8:base + 10]
                t12 = sstat[:, base + 10:base + 12]
                nm22 = sstat[:, base + 12:base + 14]
                ve2 = sstat[:, base + 14:base + 16]
                std2 = sstat[:, base + 16:base + 18]
                rs2 = sstat[:, base + 18:base + 20]
                nmb2 = sstat[:, base + 20:base + 22]
                for ti in range(2):
                    tt = 2 * pr + ti
                    par = tt % 2
                    bank = ["pb2a", "pb2b"][par]
                    pby = [pb2a, pb2b][par]
                    for oc in range(8):
                        P.add("tensor", lambda e, tt=tt, oc=oc, pby=pby: e.matmul(
                            pby[:, oc * 128:(oc + 1) * 128],
                            ybT[:, oc * 512 + tt * 128: oc * 512 + (tt + 1) * 128], sid[:],
                            start=True, stop=True),
                            waits=([("s", s_yb)] + bg(bank)) if oc == 0 else [],
                            inc="te" if oc == 7 else None)
                    tev = P.ctr["te"]
                    v_e2 = 0
                    for hf in range(2):
                        acol = sstat[:, base + ti * 2 + hf: base + ti * 2 + hf + 1]
                        v_e2 = P.add("vector", lambda e, tt=tt, hf=hf, par=par, pby=pby,
                                     acol=acol: e.scalar_tensor_tensor(
                            xft[par][:].rearrange("p (o u) -> p o u", o=8)[
                                :, hf * 4:(hf + 1) * 4, :],
                            pby[:, hf * 512:(hf + 1) * 512].rearrange("p (a u) -> p a u", a=4),
                            1.0,
                            x1n[:, tt * 1024:(tt + 1) * 1024].rearrange(
                                "p (o u) -> p o u", o=8)[:, hf * 4:(hf + 1) * 4, :],
                            ALU.mult, ALU.add, accum_out=acol),
                            waits=[("te", tev), ("s", s_prev_apply2[par])], inc="v")
                    bset(bank, "v", v_e2)
                    vdrain()
                    P.add("vector", lambda e, par=par, ti=ti, ssq2=ssq2: e.scalar_tensor_tensor(
                        x1nb[:], xft[par][:], 1.0, xft[par][:], ALU.mult, ALU.mult,
                        accum_out=ssq2[:, ti:ti + 1]),
                        waits=bg("x1nb"), inc="v")
                vdrain()
                P.add("vector", lambda e, s12=s12, h0s=h0s, h1s=h1s: e.tensor_tensor(
                    s12, h0s, h1s, ALU.add), inc="v")
                P.add("vector", lambda e, t12=t12, ssq2=ssq2: e.tensor_scalar(
                    t12, ssq2, D * 1e-5, 1.0 / D, ALU.add, ALU.mult), inc="v")
                vdrain()
                P.add("vector", lambda e, m2=m2, s12=s12: e.tensor_scalar(
                    m2, s12, 1.0 / D, None, ALU.mult), inc="v")
                vdrain()
                P.add("vector", lambda e, nm22=nm22, m2=m2: e.scalar_tensor_tensor(
                    nm22, m2, -1.0, m2, ALU.mult, ALU.mult), inc="v")
                vdrain()
                P.add("vector", lambda e, ve2=ve2, t12=t12, nm22=nm22: e.tensor_tensor(
                    ve2, t12, nm22, ALU.add), inc="v")
                vv = vdrain(inc="v")
                P.add("scalar", lambda e, std2=std2, ve2=ve2: e.activation(
                    std2, ve2, AF.Sqrt, bias=0.0, scale=1.0),
                    waits=[("v", vv)], inc="s")
                sq = sdrain(inc="s")
                P.add("vector", lambda e, rs2=rs2, std2=std2: e.reciprocal(rs2, std2),
                      waits=[("s", sq)], inc="v")
                vdrain()
                P.add("vector", lambda e, nmb2=nmb2, m2=m2, rs2=rs2: e.scalar_tensor_tensor(
                    nmb2, m2, -1.0, rs2, ALU.mult, ALU.mult), inc="v")
                vnm = vdrain(inc="v")
                for ti in range(2):
                    tt = 2 * pr + ti
                    par = tt % 2
                    rsc = rs2[:, ti:ti + 1]
                    nmbc = nmb2[:, ti:ti + 1]
                    gtile = c * NTT + tt
                    P.add("scalar", lambda e, par=par, rsc=rsc, nmbc=nmbc: e.activation(
                        x1t[par][:], xft[par][:], AF.Identity, bias=nmbc, scale=rsc),
                        waits=[("v", vnm), ("st%d" % par, x1t_st_guard[par])],
                        inc="s")
                    s_ap2 = sdrain(inc="s")
                    s_prev_apply2[par] = s_ap2
                    v_xft_guard[par] = ("s", s_ap2)
                    x1t_st_guard[par] = P.add("sync", lambda e, gtile=gtile, par=par: e.dma_start(
                        out=y_h.rearrange("(n p) d -> n p d", p=128)[gtile],
                        in_=x1t[par][:]),
                        waits=[("s", s_ap2)], inc="st%d" % par, n=16)

        P.add("sync", lambda e: e.nop(),
              waits=[("st0", P.ctr["st0"]), ("st1", P.ctr["st1"])])

        with nc.Block() as block:
            @block.sync
            def _(eng):
                P.replay("sync", eng)

            @block.gpsimd
            def _(eng):
                P.replay("gpsimd", eng)

            @block.tensor
            def _(eng):
                P.replay("tensor", eng)

            @block.vector
            def _(eng):
                P.replay("vector", eng)

            @block.scalar
            def _(eng):
                P.replay("scalar", eng)

    return nc


def host_inputs(L0_attn_W_qkv, L0_attn_b_qkv, L0_attn_W_out, L0_attn_b_out,
                L0_ffn_kernel, L0_ffn_input_scale, L0_ffn_kernel_scale, L0_ffn_bias):
    Wq = np.asarray(L0_attn_W_qkv)
    bq = np.asarray(L0_attn_b_qkv)
    Wo = np.asarray(L0_attn_W_out)
    bo = np.asarray(L0_attn_b_out)
    kern = np.asarray(L0_ffn_kernel)
    bff = np.asarray(L0_ffn_bias)

    perm = []
    for i in range(16):
        perm += list(range(i * 192, i * 192 + 64))
        perm += list(range(i * 192 + 64, i * 192 + 128))
    for i in range(8):
        perm += list(range((2 * i) * 192 + 128, (2 * i) * 192 + 192))
        perm += list(range((2 * i + 1) * 192 + 128, (2 * i + 1) * 192 + 192))
    perm = np.array(perm)

    wqkv = np.ascontiguousarray(Wq[:, perm].reshape(8, 128, 3072))
    bqkv = np.ascontiguousarray(bq[perm].reshape(24, 128).T)
    wout = np.ascontiguousarray(np.stack(
        [np.concatenate([Wo[64 * j:64 * j + 64], Wo[64 * (j + 8):64 * (j + 8) + 64]],
                        axis=0) for j in range(8)]))
    kern_r = np.ascontiguousarray(kern.reshape(8, 128, 1024))
    bout = np.ascontiguousarray(bo.reshape(8, 128).T)
    bffn = np.ascontiguousarray(bff.reshape(8, 128).T)
    sin = np.ascontiguousarray(np.asarray(L0_ffn_input_scale).reshape(1, 1).astype(np.float32))
    sk = np.ascontiguousarray(np.asarray(L0_ffn_kernel_scale).reshape(1, 1).astype(np.float32))

    ident = np.eye(128, dtype=ml_dtypes.bfloat16)
    p8c = np.zeros((8, 128), dtype=ml_dtypes.bfloat16)
    for cc in range(8):
        p8c[cc, np.arange(128) % 8 == cc] = 8.0
    ones1 = np.ones((1, 128), dtype=np.float32)

    return dict(wqkv=wqkv, bqkv=bqkv, wout=wout, kern=kern_r, bout=bout,
                bffn=bffn, sin=sin, sk=sk, ident=ident, p8c=p8c, ones1=ones1)


def kernel(L0_attn_x, L0_attn_W_qkv, L0_attn_b_qkv, L0_attn_W_out, L0_attn_b_out,
           L0_ffn_kernel, L0_ffn_input_scale, L0_ffn_kernel_scale, L0_ffn_bias):
    x = np.ascontiguousarray(np.asarray(L0_attn_x).reshape(-1, D))
    ntok = x.shape[0]
    tpc = ntok // NCORE
    shared = host_inputs(L0_attn_W_qkv, L0_attn_b_qkv, L0_attn_W_out, L0_attn_b_out,
                         L0_ffn_kernel, L0_ffn_input_scale, L0_ffn_kernel_scale,
                         L0_ffn_bias)
    nc = build_program(tpc)
    in_maps = []
    for i in range(NCORE):
        m = dict(shared)
        m["x"] = np.ascontiguousarray(x[i * tpc:(i + 1) * tpc])
        in_maps.append(m)
    res = run_bass_kernel_spmd(nc, in_maps, core_ids=list(range(NCORE)))
    out = np.concatenate([np.asarray(r["y"]) for r in res.results], axis=0)
    return out.reshape(np.asarray(L0_attn_x).shape).astype(np.float32)


# version 17
# speedup vs baseline: 1.0959x; 1.0648x over previous
import contextlib
import numpy as np
import ml_dtypes
import concourse.bass as bass
import concourse.mybir as mybir
from concourse.bass_utils import run_bass_kernel_spmd

F32 = mybir.dt.float32
BF16 = mybir.dt.bfloat16
FP8 = mybir.dt.float8e4
AX = mybir.AxisListType
AF = mybir.ActivationFunctionType
ALU = mybir.AluOpType

H = 16
DK = 64
D = 1024
TC = 512
NTT = TC // 128
NB = TC // 64
NCORE = 8

RIN, RSK, SPROD = 0, 1, 2

OWN = {"tensor": "te", "vector": "v", "scalar": "s", "sync": None, "gpsimd": None}


class Prog:

    def __init__(self, nc, stack):
        self.nc = nc
        self.items = {e: [] for e in OWN}
        self.ctr = {}
        self.sems = {}
        for name in ("te", "v", "s", "ld", "lf0", "lf1", "gw", "gk",
                     "gb0", "gb1", "gb2", "gb3", "gr", "gd", "st0", "st1"):
            self.sems[name] = stack.enter_context(nc.semaphore(name))
            self.ctr[name] = 0

    def add(self, eng, fn, waits=(), inc=None, n=1, selfw=False):
        w = [(s, v) for (s, v) in waits if v > 0]
        if inc is None and OWN[eng] in ("v", "s"):
            inc = OWN[eng]
        self.items[eng].append((w, fn, inc, n, selfw))
        if inc is not None:
            self.ctr[inc] += n
            return self.ctr[inc]
        return None

    def replay(self, engname, engobj):
        waited = {}
        own = OWN[engname]
        for waits, fn, inc, n, selfw in self.items[engname]:
            for sname, val in waits:
                if sname == own and not selfw:
                    continue
                if waited.get(sname, 0) >= val:
                    continue
                engobj.wait_ge(self.sems[sname], val)
                waited[sname] = val
            ins = fn(engobj)
            if inc is not None:
                ins.then_inc(self.sems[inc], n)


def build_program(T, dbg=False):
    NCH = T // TC
    nc = bass.Bass("TRN2", target_bir_lowering=False, debug=False)

    x_h = nc.dram_tensor("x", [T, D], F32, kind="ExternalInput").ap()
    wqkv_h = nc.dram_tensor("wqkv", [8, 128, 3072], F32, kind="ExternalInput").ap()
    wout_h = nc.dram_tensor("wout", [8, 128, 1024], F32, kind="ExternalInput").ap()
    kern_h = nc.dram_tensor("kern", [8, 128, 1024], BF16, kind="ExternalInput").ap()
    bqkv_h = nc.dram_tensor("bqkv", [128, 24], F32, kind="ExternalInput").ap()
    bout_h = nc.dram_tensor("bout", [128, 8], F32, kind="ExternalInput").ap()
    bffn_h = nc.dram_tensor("bffn", [128, 8], BF16, kind="ExternalInput").ap()
    sin_h = nc.dram_tensor("sin", [1, 1], F32, kind="ExternalInput").ap()
    sk_h = nc.dram_tensor("sk", [1, 1], F32, kind="ExternalInput").ap()
    ident_h = nc.dram_tensor("ident", [128, 128], BF16, kind="ExternalInput").ap()
    p8_h = nc.dram_tensor("p8c", [8, 128], BF16, kind="ExternalInput").ap()
    ones_h = nc.dram_tensor("ones1", [1, 128], F32, kind="ExternalInput").ap()
    y_h = nc.dram_tensor("y", [T, D], F32, kind="ExternalOutput").ap()

    dbg_h = {}
    if dbg:
        for name, shape, dt in [
            ("d_q", [64, 8192], BF16), ("d_k", [128, 8192], BF16),
            ("d_v", [128, 8192], BF16), ("d_e", [128, 1024], BF16),
            ("d_den", [128, 8], F32), ("d_at", [128, 1024], BF16),
            ("d_vt", [128, 1024], BF16), ("d_ctx", [128, 4096], BF16),
            ("d_x1", [128, 4096], F32), ("d_x1n", [128, 4096], F32),
            ("d_xq", [128, 4096], FP8), ("d_yb", [128, 4096], BF16),
            ("d_an", [128, 1024], BF16), ("d_stat", [128, 88], F32),
            ("d_xt", [128, 4096], BF16),
        ]:
            dbg_h[name] = nc.dram_tensor(name, shape, dt, kind="ExternalOutput").ap()

    st = contextlib.ExitStack()
    en = st.enter_context
    with st:
        swq = en(nc.sbuf_tensor("swq", [128, 24576], BF16))
        swo = en(nc.sbuf_tensor("swo", [128, 8192], BF16))
        skq = en(nc.sbuf_tensor("skq", [128, 8192], FP8))
        sbq = en(nc.sbuf_tensor("sbq", [128, 24], F32))
        sbo = en(nc.sbuf_tensor("sbo", [128, 8], F32))
        sbfb = en(nc.sbuf_tensor("sbfb", [128, 8], BF16))
        sbf32 = en(nc.sbuf_tensor("sbf32", [128, 8], F32))
        sid = en(nc.sbuf_tensor("sid", [128, 128], BF16))
        sp8 = en(nc.sbuf_tensor("sp8", [8, 128], BF16))
        sones = en(nc.sbuf_tensor("sones", [1, 128], F32))
        ssc = en(nc.sbuf_tensor("ssc", [1, 4], F32))
        ssc2 = en(nc.sbuf_tensor("ssc2", [1, 4], F32))
        srcp = en(nc.sbuf_tensor("srcp", [128, 4], F32))
        szero = en(nc.sbuf_tensor("szero", [128, 1], F32))
        negc = en(nc.sbuf_tensor("negc", [128, 1], F32))
        qbuf = en(nc.sbuf_tensor("qbuf", [64, 8192], BF16))
        kbuf = en(nc.sbuf_tensor("kbuf", [128, 8192], BF16))
        vbuf = en(nc.sbuf_tensor("vbuf", [128, 8192], BF16))
        xT = en(nc.sbuf_tensor("xT", [128, 4096], BF16))
        xb = [en(nc.sbuf_tensor(f"xb{i}", [128, 1024], BF16)) for i in range(4)]
        xft = [en(nc.sbuf_tensor(f"xft{i}", [128, 1024], F32)) for i in range(2)]
        x1t = [en(nc.sbuf_tensor(f"x1t{i}", [128, 1024], F32)) for i in range(2)]
        sE = en(nc.sbuf_tensor("sE", [128, 1024], BF16))
        sAttn = en(nc.sbuf_tensor("sAttn", [128, 1024], BF16))
        sAT = en(nc.sbuf_tensor("sAT", [128, 1024], BF16))
        sVT = en(nc.sbuf_tensor("sVT", [128, 1024], BF16))
        sden = en(nc.sbuf_tensor("sden", [128, 8], F32))
        sdrc = en(nc.sbuf_tensor("sdrc", [128, 8], F32))
        ctx = en(nc.sbuf_tensor("ctx", [128, 4096], BF16))
        aoTc = en(nc.sbuf_tensor("aoTc", [128, 4096], BF16))
        x1n = en(nc.sbuf_tensor("x1n", [128, 4096], F32))
        x1nb = en(nc.sbuf_tensor("x1nb", [128, 1024], BF16))
        xqT = en(nc.sbuf_tensor("xqT", [128, 4096], FP8))
        ybT = en(nc.sbuf_tensor("ybT", [128, 4096], BF16))
        sstat = en(nc.sbuf_tensor("sstat", [128, 88], F32))

        pb2a = en(nc.psum_tensor("pb2a", [128, 1024], F32))
        pb2b = en(nc.psum_tensor("pb2b", [128, 1024], F32))
        pb2c = en(nc.psum_tensor("pb2c", [128, 1024], F32))
        pb1a = en(nc.psum_tensor("pb1a", [128, 512], F32))
        pb1b = en(nc.psum_tensor("pb1b", [128, 512], F32))

        P = Prog(nc, st)

        bguard = {}

        def bset(name, sem, val):
            bguard[name] = [(sem, val)]

        def badd(name, sem, val):
            bguard.setdefault(name, []).append((sem, val))

        def bg(*names):
            out = []
            for nm in names:
                out += bguard.get(nm, [])
            return out

        def vdrain(inc=None):
            return P.add("vector", lambda e: e.drain(),
                         waits=[("v", P.ctr["v"])], selfw=True, inc=inc)

        def sdrain(inc=None):
            return P.add("scalar", lambda e: e.drain(),
                         waits=[("s", P.ctr["s"])], selfw=True, inc=inc)

        P.add("sync", lambda e: e.dma_start(out=sid[:], in_=ident_h[:]), inc="ld", n=16)
        P.add("sync", lambda e: e.dma_start(out=sp8[:], in_=p8_h[:]), inc="ld", n=16)
        P.add("sync", lambda e: e.dma_start(out=sones[:], in_=ones_h[:]), inc="ld", n=16)
        P.add("sync", lambda e: e.dma_start(out=sbq[:], in_=bqkv_h[:]), inc="ld", n=16)
        P.add("sync", lambda e: e.dma_start(out=sbo[:], in_=bout_h[:]), inc="ld", n=16)
        P.add("sync", lambda e: e.dma_start(out=sbfb[:], in_=bffn_h[:]), inc="ld", n=16)
        P.add("sync", lambda e: e.dma_start(out=ssc[0:1, 0:1], in_=sin_h[:]), inc="ld", n=16)
        ld_consts = P.add("sync", lambda e: e.dma_start(out=ssc[0:1, 1:2], in_=sk_h[:]),
                          inc="ld", n=16)

        P.add("gpsimd", lambda e: e.dma_start(
            out=swq[:].rearrange("p (c f) -> p c f", c=8), in_=wqkv_h.rearrange("c p f -> p c f")), inc="gw", n=16)
        gp_w = P.add("gpsimd", lambda e: e.dma_start(
            out=swo[:].rearrange("p (j f) -> p j f", j=8), in_=wout_h.rearrange("j p f -> p j f")), inc="gw", n=16)

        P.add("vector", lambda e: e.tensor_copy(sbf32[:], sbfb[:]),
              waits=[("ld", ld_consts)], inc="v")
        P.add("vector", lambda e: e.reciprocal(ssc2[0:1, 0:2], ssc[0:1, 0:2]), inc="v")
        P.add("vector", lambda e: e.tensor_mul(
            ssc2[0:1, 2:3], ssc[0:1, 0:1], ssc[0:1, 1:2]), inc="v")
        v_rc = vdrain(inc="v")
        for k in range(3):
            P.add("tensor", lambda e, k=k: e.matmul(
                pb1a[:, k:k + 1], sones[0:1, :], ssc2[0:1, k:k + 1], start=True, stop=True),
                waits=[("v", v_rc)], inc="te" if k == 2 else None)
        te_bc = P.ctr["te"]
        P.add("vector", lambda e: e.tensor_copy(srcp[:, 0:3], pb1a[:, 0:3]),
              waits=[("te", te_bc)], inc="v")
        v_srcp = vdrain(inc="v")
        bset("pb1a", "v", v_srcp)
        P.add("vector", lambda e: e.memset(szero[:], 0.0), inc="v")
        P.add("vector", lambda e: e.memset(vbuf[:], 0.0), inc="v")
        v_vz = vdrain(inc="v")

        s_q = 0
        for cc in range(8):
            ldv = P.add("gpsimd", lambda e, cc=cc: e.dma_start(out=x1nb[:], in_=kern_h[cc]),
                        waits=[("s", s_q)], inc="gk", n=16)
            s_q = P.add("scalar", lambda e, cc=cc: e.activation(
                skq[:, cc * 1024:(cc + 1) * 1024], x1nb[:], AF.Identity, bias=0.0,
                scale=srcp[:, RSK:RSK + 1]),
                waits=[("gk", ldv), ("v", v_srcp)], inc="s")
        bset("x1nb", "s", s_q)

        def q_dest(h):
            return qbuf[0:64, :].rearrange("p (G u) -> p G u", G=64)[:, :, h * 8:(h + 1) * 8]

        def v_dest(g):
            rh = g % 2
            return vbuf[rh * 64:(rh + 1) * 64, :].rearrange(
                "p (G u) -> p G u", G=64)[:, :, g * 8:(g + 1) * 8]

        v_negc = 0
        s_prev_apply1 = [0, 0]
        s_prev_apply2 = [0, 0]
        v_xft_guard = [("v", 0), ("v", 0)]
        x1t_st_guard = [0, 0]
        st_out = 0

        for c in range(NCH):
            v_xT_done = 0
            for tt in range(NTT):
                ldb = P.add("gpsimd", lambda e, c=c, tt=tt: e.dma_start(
                    out=xb[tt % 4][:],
                    in_=x_h.rearrange("(n p) d -> n p d", p=128)[c * NTT + tt]),
                    waits=bg(("xb", tt % 4)), inc="gb%d" % (tt % 4), n=16)
                bank = ["pb2a", "pb2b"][tt % 2]
                pbx = [pb2a, pb2b][tt % 2]
                for ec in range(8):
                    P.add("tensor", lambda e, tt=tt, ec=ec, pbx=pbx: e.matmul(
                        pbx[:, ec * 128:(ec + 1) * 128],
                        xb[tt % 4][:, ec * 128:(ec + 1) * 128], sid[:],
                        start=True, stop=True),
                        waits=([("gb%d" % (tt % 4), ldb), ("ld", ld_consts)] + bg(bank))
                        if ec == 0 else [],
                        inc="te" if ec == 7 else None)
                tev = P.ctr["te"]
                bset(("xb", tt % 4), "te", tev)
                for hf in range(2):
                    v_xT_done = P.add("vector", lambda e, tt=tt, hf=hf, pbx=pbx: e.tensor_copy(
                        xT[:].rearrange("p (ec u) -> p ec u", ec=8)[
                            :, hf * 4:(hf + 1) * 4, tt * 128:(tt + 1) * 128],
                        pbx[:, hf * 512:(hf + 1) * 512].rearrange("p (a u) -> p a u", a=4)),
                        waits=[("te", tev)], inc="v")
                bset(bank, "v", v_xT_done)

            v_evac = 0
            s_evac = 0
            for oc in range(24):
                bank = ["pb1a", "pb1b"][oc % 2]
                pbq = [pb1a, pb1b][oc % 2]
                for kc in range(8):
                    P.add("tensor", lambda e, oc=oc, kc=kc, pbq=pbq: e.matmul(
                        pbq[:], swq[:, kc * 3072 + oc * 128: kc * 3072 + (oc + 1) * 128],
                        xT[:, kc * 512:(kc + 1) * 512],
                        start=(kc == 0), stop=(kc == 7)),
                        waits=([("v", v_xT_done), ("gw", gp_w)] + bg(bank))
                        if kc == 0 else [],
                        inc="te" if kc == 7 else None)
                tev = P.ctr["te"]
                guards = []
                if oc < 16:
                    hq = oc
                    v_evac = P.add("vector", lambda e, oc=oc, hq=hq, pbq=pbq: e.tensor_scalar(
                        q_dest(hq),
                        pbq[0:64, :].rearrange("p (G t) -> p G t", G=64),
                        sbq[0:64, oc:oc + 1], 0.125, ALU.add, ALU.mult),
                        waits=[("te", tev)], inc="v")
                    guards.append(("v", v_evac))
                    gk = oc
                    if oc < 8:
                        v_evac = P.add("vector", lambda e, oc=oc, gk=gk, pbq=pbq: e.tensor_scalar(
                            kbuf[64:128, gk * 512:(gk + 1) * 512], pbq[64:128, :],
                            sbq[64:128, oc:oc + 1], None, ALU.add),
                            waits=[("te", tev)], inc="v")
                        guards.append(("v", v_evac))
                    else:
                        s_evac = P.add("scalar", lambda e, oc=oc, gk=gk, pbq=pbq: e.activation(
                            kbuf[64:128, gk * 512:(gk + 1) * 512], pbq[64:128, :],
                            AF.Identity, bias=sbq[64:128, oc:oc + 1], scale=1.0),
                            waits=[("te", tev)], inc="s")
                        guards.append(("s", s_evac))
                else:
                    i = oc - 16
                    for hf, g in ((0, 2 * i), (1, 2 * i + 1)):
                        s_evac = P.add("scalar", lambda e, hf=hf, g=g, oc=oc, pbq=pbq: e.activation(
                            v_dest(g),
                            pbq[hf * 64:(hf + 1) * 64, :].rearrange("p (G t) -> p G t", G=64),
                            AF.Identity, bias=sbq[hf * 64:(hf + 1) * 64, oc:oc + 1], scale=1.0),
                            waits=[("te", tev), ("v", v_vz)], inc="s")
                        guards.append(("s", s_evac))
                bguard[bank] = guards
            v_qkv_evac = vdrain(inc="v")
            s_qkv_evac = sdrain(inc="s")
            gp_rep = P.add("gpsimd", lambda e: e.dma_start(
                out=kbuf[0:64, :], in_=kbuf[64:128, :]),
                waits=[("v", v_qkv_evac), ("s", s_qkv_evac)], inc="gr", n=16)

            if dbg and c == 0:
                P.add("gpsimd", lambda e: e.dma_start(out=dbg_h["d_xt"][:], in_=xT[:]),
                      waits=[("v", v_xT_done)], inc="gd", n=16)
                P.add("gpsimd", lambda e: e.dma_start(out=dbg_h["d_q"][:], in_=qbuf[:]),
                      waits=[("v", v_qkv_evac), ("s", s_qkv_evac)], inc="gd", n=16)
                P.add("gpsimd", lambda e: e.dma_start(out=dbg_h["d_k"][:], in_=kbuf[:]),
                      waits=[("gr", gp_rep)], inc="gd", n=16)
                P.add("gpsimd", lambda e: e.dma_start(out=dbg_h["d_v"][:], in_=vbuf[:]),
                      waits=[("v", v_qkv_evac), ("s", s_qkv_evac)], inc="gd", n=16)

            s_exp = 0
            v_attn = 0
            s_attn = 0
            te_at = 0
            te_mm3 = 0
            s_ctx_evac = 0
            v_dbg_den = 0
            gp_dbg_atvt = 0
            def emit_mm1(b, entry):
                for gl in range(8):
                    G = b * 8 + gl
                    P.add("tensor", lambda e, gl=gl: e.matmul(
                        pb2a[:, gl * 128:(gl + 1) * 128], sp8[0:8, :], sp8[0:8, :],
                        start=True, stop=False),
                        waits=(([("v", v_qkv_evac), ("s", s_qkv_evac), ("gr", gp_rep)]
                                if entry else []) + bg("pb2a")) if gl == 0 else [])
                    P.add("tensor", lambda e, G=G, gl=gl: e.matmul(
                        pb2a[:, gl * 128:(gl + 1) * 128],
                        qbuf[:, G * 128:(G + 1) * 128],
                        kbuf[0:64, :].rearrange("p (g u) -> p g u", g=16)[
                            :, :, G * 8:(G + 1) * 8],
                        start=False, stop=True),
                        inc="te" if gl == 7 else None)
                return P.ctr["te"]

            def emit_ctx(bprev):
                nonlocal_ns = None
                return bprev

            pend_ctx = None
            v_ctx = 0
            te_mm1_cur = emit_mm1(0, True)
            for b in range(NB):
                v_dbg_den = v_dbg_den if b else 0
                te_mm1 = te_mm1_cur
                if c == 0 and b == 0:
                    P.add("vector", lambda e: e.reduce_max(
                        negc[:, 0:1], pb2a[:], axis=AX.X, negate=True),
                        waits=[("te", te_mm1)], inc="v")
                    v_negc = vdrain(inc="v")
                s_exp = P.add("scalar", lambda e: e.activation(
                    sE[:], pb2a[:], AF.Exp, bias=negc[:, 0:1], scale=1.0),
                    waits=[("te", te_mm1), ("v", v_negc), ("v", v_attn)],
                    inc="s")
                bset("pb2a", "s", s_exp)
                gp_dbg = 0
                if dbg and c == 0 and b == 0:
                    gp_dbg = P.add("gpsimd", lambda e: e.dma_start(out=dbg_h["d_e"][:], in_=sE[:]),
                                   waits=[("s", s_exp)], inc="gd", n=16)
                P.add("vector", lambda e: e.reduce_sum(
                    sden[:, 0:8], sE[:].rearrange("p (g u) -> p g u", g=8), axis=AX.X),
                    waits=[("s", s_exp), ("gd", v_dbg_den), ("gd", gp_dbg)], inc="v")
                if dbg and c == 0 and b == 0:
                    vden = P.ctr["v"]
                    gp_dbg = P.add("gpsimd", lambda e: e.dma_start(out=dbg_h["d_den"][:], in_=sden[:]),
                                   waits=[("v", vden)], inc="gd", n=16)
                    v_dbg_den_next = gp_dbg
                else:
                    v_dbg_den_next = 0
                v_dbg_den = v_dbg_den_next
                P.add("vector", lambda e: e.drain())
                P.add("vector", lambda e: e.reciprocal(sdrc[:, 0:8], sden[:, 0:8]),
                      waits=[("s", s_attn)], inc="v")
                v_rcp = vdrain(inc="v")
                if pend_ctx is not None and pend_ctx % 2 == 0:
                    v_ctx = P.add("vector", lambda e, bp=pend_ctx: e.tensor_tensor(
                        ctx[:, bp * 512:(bp + 1) * 512], pb1a[:],
                        szero[:, 0:1].broadcast_to([128, 512]), ALU.add),
                        waits=[("te", te_mm3)], inc="v")
                    bguard["pb1a"] = [("v", v_ctx)]
                    pend_ctx = None
                v_attn = P.add("vector", lambda e: e.tensor_tensor(
                    sAttn[:, 0:640].rearrange("p (g u) -> p g u", g=5),
                    sE[:, 0:640].rearrange("p (g u) -> p g u", g=5),
                    sdrc[:, 0:5].rearrange("p (g o) -> p g o", o=1).broadcast_to([128, 5, 128]),
                    ALU.mult), waits=[("gd", gp_dbg), ("te", te_at)], inc="v")
                s_attn = 0
                for g5 in range(5, 8):
                    s_attn = P.add("scalar", lambda e, g5=g5: e.activation(
                        sAttn[:, g5 * 128:(g5 + 1) * 128],
                        sE[:, g5 * 128:(g5 + 1) * 128], AF.Identity,
                        bias=0.0, scale=sdrc[:, g5:g5 + 1]),
                        waits=[("v", v_rcp), ("te", te_at)] if g5 == 5 else [],
                        inc="s")
                gp_dbg_an = 0
                if dbg and c == 0 and b == 0:
                    gp_dbg_an = P.add("gpsimd", lambda e: e.dma_start(
                        out=dbg_h["d_an"][:], in_=sAttn[:]),
                        waits=[("v", v_attn), ("s", s_attn)], inc="gd", n=16)
                for gl in range(8):
                    G = b * 8 + gl
                    P.add("tensor", lambda e, G=G, gl=gl: e.matmul(
                        pb2c[:, gl * 128:(gl + 1) * 128],
                        vbuf[:, G * 128:(G + 1) * 128], sid[:],
                        start=True, stop=True),
                        waits=([("v", v_qkv_evac), ("s", s_qkv_evac)] + bg("pb2c"))
                        if gl == 0 else [],
                        inc="te" if gl == 7 else None)
                te_vt = P.ctr["te"]
                if b + 1 < NB:
                    te_mm1_cur = emit_mm1(b + 1, False)
                s_vt = P.add("scalar", lambda e: e.copy(
                    sVT[:, 0:512], pb2c[:, 0:512]),
                    waits=[("te", te_vt), ("te", te_mm3), ("gd", gp_dbg_atvt)], inc="s")
                s_vt = P.add("scalar", lambda e: e.copy(
                    sVT[:, 512:1024], pb2c[:, 512:1024]), inc="s")
                bguard["pb2c"] = [("s", s_vt)]
                if pend_ctx is not None and pend_ctx % 2 == 1:
                    s_ctx_evac = P.add("scalar", lambda e, bp=pend_ctx: e.copy(
                        ctx[:, bp * 512:(bp + 1) * 512], pb1a[:]),
                        waits=[("te", te_mm3)], inc="s")
                    bguard["pb1a"] = [("s", s_ctx_evac)]
                    pend_ctx = None
                for gl in range(8):
                    P.add("tensor", lambda e, gl=gl: e.matmul(
                        pb2b[:, gl * 128:(gl + 1) * 128],
                        sAttn[:, gl * 128:(gl + 1) * 128], sid[:],
                        start=True, stop=True),
                        waits=([("v", v_attn), ("gd", gp_dbg_an)] + bg("pb2b"))
                        if gl == 0 else ([("s", s_attn)] if gl == 5 else []),
                        inc="te" if gl == 7 else None)
                te_at = P.ctr["te"]
                v_at = P.add("vector", lambda e: e.tensor_tensor(
                    sAT[:, 0:512], pb2b[:, 0:512],
                    szero[:, 0:1].broadcast_to([128, 512]), ALU.add),
                    waits=[("te", te_at), ("te", te_mm3), ("gd", gp_dbg_atvt)], inc="v")
                v_at = P.add("vector", lambda e: e.tensor_tensor(
                    sAT[:, 512:1024], pb2b[:, 512:1024],
                    szero[:, 0:1].broadcast_to([128, 512]), ALU.add), inc="v")
                bguard["pb2b"] = [("v", v_at)]
                if dbg and c == 0 and b == 0:
                    P.add("gpsimd", lambda e: e.dma_start(out=dbg_h["d_at"][:], in_=sAT[:]),
                          waits=[("v", v_at)], inc="gd", n=16)
                    gp_dbg_atvt = P.add("gpsimd", lambda e: e.dma_start(
                        out=dbg_h["d_vt"][:], in_=sVT[:]),
                        waits=[("s", s_vt)], inc="gd", n=16)
                for gl in range(8):
                    for bb in range(2):
                        for pp in range(2):
                            P.add("tensor", lambda e, gl=gl, bb=bb, pp=pp: e.matmul(
                                pb1a[bb * 64:(bb + 1) * 64, gl * 64:(gl + 1) * 64],
                                sVT[:, gl * 128 + pp * 64: gl * 128 + (pp + 1) * 64],
                                sAT[:, gl * 128 + bb * 64: gl * 128 + (bb + 1) * 64],
                                start=(pp == 0), stop=(pp == 1),
                                tile_position=(0, bb * 64)),
                                waits=([("v", v_at), ("s", s_vt)] + bg("pb1a"))
                                if (gl == 0 and bb == 0 and pp == 0) else [],
                                inc="te" if (gl == 7 and bb == 1 and pp == 1) else None)
                te_mm3 = P.ctr["te"]
                pend_ctx = b
            s_ctx_evac = P.add("scalar", lambda e, bp=pend_ctx: e.copy(
                ctx[:, bp * 512:(bp + 1) * 512], pb1a[:]),
                waits=[("te", te_mm3)], inc="s")
            bguard["pb1a"] = [("s", s_ctx_evac)]
            pend_ctx = None
            if dbg and c == 0:
                P.add("gpsimd", lambda e: e.dma_start(out=dbg_h["d_ctx"][:], in_=ctx[:]),
                      waits=[("s", s_ctx_evac)], inc="gd", n=16)

            s_ao = 0
            for oc in range(8):
                bank = ["pb1b", "pb1a"][oc % 2]
                pbw = [pb1b, pb1a][oc % 2]
                for j in range(8):
                    P.add("tensor", lambda e, oc=oc, j=j, pbw=pbw: e.matmul(
                        pbw[:], swo[:, j * 1024 + oc * 128: j * 1024 + (oc + 1) * 128],
                        ctx[:].rearrange("p (G u) -> p G u", G=64)[:, :, j * 8:(j + 1) * 8],
                        start=(j == 0), stop=(j == 7)),
                        waits=([("s", s_ctx_evac), ("v", v_ctx)] + bg(bank))
                        if j == 0 else [],
                        inc="te" if j == 7 else None)
                tev = P.ctr["te"]
                s_ao = P.add("scalar", lambda e, oc=oc, pbw=pbw: e.activation(
                    aoTc[:, oc * 512:(oc + 1) * 512], pbw[:], AF.Identity,
                    bias=sbo[:, oc:oc + 1], scale=1.0),
                    waits=[("te", tev)], inc="s")
                bset(bank, "s", s_ao)

            s_quant = 0
            for pr in range(2):
                base = pr * 22
                h0s = sstat[:, base:base + 4].rearrange("p (t h) -> p t h", t=2)[:, :, 0:1]
                h1s = sstat[:, base:base + 4].rearrange("p (t h) -> p t h", t=2)[:, :, 1:2]
                ssq2 = sstat[:, base + 4:base + 6]
                s12 = sstat[:, base + 6:base + 8]
                m2 = sstat[:, base + 8:base + 10]
                t12 = sstat[:, base + 10:base + 12]
                nm22 = sstat[:, base + 12:base + 14]
                ve2 = sstat[:, base + 14:base + 16]
                std2 = sstat[:, base + 16:base + 18]
                rs2 = sstat[:, base + 18:base + 20]
                nmb2 = sstat[:, base + 20:base + 22]
                for ti in range(2):
                    tt = 2 * pr + ti
                    par = tt % 2
                    ldf = P.add("sync", lambda e, c=c, tt=tt, par=par: e.dma_start(
                        out=xft[par][:],
                        in_=x_h.rearrange("(n p) d -> n p d", p=128)[c * NTT + tt]),
                        waits=[v_xft_guard[par]], inc="lf%d" % par, n=16)
                    for oc in range(8):
                        P.add("tensor", lambda e, tt=tt, oc=oc: e.matmul(
                            pb2a[:, oc * 128:(oc + 1) * 128],
                            aoTc[:, oc * 512 + tt * 128: oc * 512 + (tt + 1) * 128], sid[:],
                            start=True, stop=True),
                            waits=([("s", s_ao)] + bg("pb2a")) if oc == 0 else [],
                            inc="te" if oc == 7 else None)
                    tev = P.ctr["te"]
                    v_ev2 = 0
                    for hf in range(2):
                        acol = sstat[:, base + ti * 2 + hf: base + ti * 2 + hf + 1]
                        v_ev2 = P.add("vector", lambda e, hf=hf, par=par, acol=acol: e.scalar_tensor_tensor(
                            x1t[par][:].rearrange("p (o u) -> p o u", o=8)[
                                :, hf * 4:(hf + 1) * 4, :],
                            pb2a[:, hf * 512:(hf + 1) * 512].rearrange("p (a u) -> p a u", a=4),
                            1.0,
                            xft[par][:].rearrange("p (o u) -> p o u", o=8)[
                                :, hf * 4:(hf + 1) * 4, :],
                            ALU.mult, ALU.add, accum_out=acol),
                            waits=[("te", tev), ("lf%d" % par, ldf),
                                   ("s", s_prev_apply1[par]),
                                   ("st%d" % par, x1t_st_guard[par])],
                            inc="v")
                    bset("pb2a", "v", v_ev2)
                    v_xft_guard[par] = ("v", v_ev2)
                    vdrain()
                    P.add("vector", lambda e, par=par, ti=ti, ssq2=ssq2: e.scalar_tensor_tensor(
                        x1nb[:], x1t[par][:], 1.0, x1t[par][:], ALU.mult, ALU.mult,
                        accum_out=ssq2[:, ti:ti + 1]),
                        waits=bg("x1nb"), inc="v")
                vdrain()
                P.add("vector", lambda e, s12=s12, h0s=h0s, h1s=h1s: e.tensor_tensor(
                    s12, h0s, h1s, ALU.add), inc="v")
                P.add("vector", lambda e, t12=t12, ssq2=ssq2: e.tensor_scalar(
                    t12, ssq2, D * 1e-5, 1.0 / D, ALU.add, ALU.mult), inc="v")
                vdrain()
                P.add("vector", lambda e, m2=m2, s12=s12: e.tensor_scalar(
                    m2, s12, 1.0 / D, None, ALU.mult), inc="v")
                vdrain()
                P.add("vector", lambda e, nm22=nm22, m2=m2: e.scalar_tensor_tensor(
                    nm22, m2, -1.0, m2, ALU.mult, ALU.mult), inc="v")
                vdrain()
                P.add("vector", lambda e, ve2=ve2, t12=t12, nm22=nm22: e.tensor_tensor(
                    ve2, t12, nm22, ALU.add), inc="v")
                vv = vdrain(inc="v")
                P.add("scalar", lambda e, std2=std2, ve2=ve2: e.activation(
                    std2, ve2, AF.Sqrt, bias=0.0, scale=1.0),
                    waits=[("v", vv)], inc="s")
                sq = sdrain(inc="s")
                P.add("vector", lambda e, rs2=rs2, std2=std2: e.reciprocal(rs2, std2),
                      waits=[("s", sq)], inc="v")
                vdrain()
                P.add("vector", lambda e, nmb2=nmb2, m2=m2, rs2=rs2: e.scalar_tensor_tensor(
                    nmb2, m2, -1.0, rs2, ALU.mult, ALU.mult), inc="v")
                vnm = vdrain(inc="v")
                for ti in range(2):
                    tt = 2 * pr + ti
                    par = tt % 2
                    rsc = rs2[:, ti:ti + 1]
                    nmbc = nmb2[:, ti:ti + 1]
                    s_ap = P.add("scalar", lambda e, tt=tt, par=par, rsc=rsc, nmbc=nmbc: e.activation(
                        x1n[:, tt * 1024:(tt + 1) * 1024], x1t[par][:],
                        AF.Identity, bias=nmbc, scale=rsc),
                        waits=[("v", vnm)], inc="s")
                    s_prev_apply1[par] = s_ap
                    vcb = P.add("scalar", lambda e, tt=tt: e.activation(
                        x1nb[:], x1n[:, tt * 1024:(tt + 1) * 1024],
                        AF.Identity, bias=0.0, scale=1.0),
                        waits=bg("x1nb"), inc="s")
                    for ec in range(8):
                        P.add("tensor", lambda e, ec=ec: e.matmul(
                            pb2b[:, ec * 128:(ec + 1) * 128],
                            x1nb[:, ec * 128:(ec + 1) * 128], sid[:],
                            start=True, stop=True),
                            waits=([("s", vcb)] + bg("pb2b")) if ec == 0 else [],
                            inc="te" if ec == 7 else None)
                    tevq = P.ctr["te"]
                    bset("x1nb", "te", tevq)
                    for hf in range(2):
                        s_quant = P.add("scalar", lambda e, tt=tt, hf=hf: e.activation(
                            xqT[:].rearrange("p (ec u) -> p ec u", ec=8)[
                                :, hf * 4:(hf + 1) * 4, tt * 128:(tt + 1) * 128],
                            pb2b[:, hf * 512:(hf + 1) * 512].rearrange("p (a u) -> p a u", a=4),
                            AF.Identity, bias=0.0, scale=srcp[:, RIN:RIN + 1]),
                            waits=[("te", tevq)], inc="s")
                    bset("pb2b", "s", s_quant)
            if dbg and c == 0:
                P.add("gpsimd", lambda e: e.dma_start(out=dbg_h["d_stat"][:], in_=sstat[:]),
                      waits=[("s", P.ctr["s"]), ("v", P.ctr["v"])], inc="gd", n=16)
                P.add("gpsimd", lambda e: e.dma_start(out=dbg_h["d_x1n"][:], in_=x1n[:]),
                      waits=[("s", P.ctr["s"])], inc="gd", n=16)
                P.add("gpsimd", lambda e: e.dma_start(out=dbg_h["d_xq"][:], in_=xqT[:]),
                      waits=[("s", s_quant)], inc="gd", n=16)
                P.add("gpsimd", lambda e: e.dma_start(out=dbg_h["d_x1"][:], in_=x1n[:]),
                      waits=[("s", P.ctr["s"])], inc="gd", n=16)

            s_yb = 0
            for oc in range(8):
                bank = ["pb1a", "pb1b"][oc % 2]
                pbf = [pb1a, pb1b][oc % 2]
                for ec in range(8):
                    P.add("tensor", lambda e, oc=oc, ec=ec, pbf=pbf: e.matmul(
                        pbf[:], skq[:, ec * 1024 + oc * 128: ec * 1024 + (oc + 1) * 128],
                        xqT[:, ec * 512:(ec + 1) * 512],
                        start=(ec == 0), stop=(ec == 7)),
                        waits=([("s", s_quant)] + bg(bank)) if ec == 0 else [],
                        inc="te" if ec == 7 else None)
                tev = P.ctr["te"]
                s_yb = P.add("scalar", lambda e, oc=oc, pbf=pbf: e.activation(
                    ybT[:, oc * 512:(oc + 1) * 512], pbf[:], AF.Identity,
                    bias=sbf32[:, oc:oc + 1], scale=srcp[:, SPROD:SPROD + 1]),
                    waits=[("te", tev)], inc="s")
                bset(bank, "s", s_yb)
            if dbg and c == 0:
                P.add("gpsimd", lambda e: e.dma_start(out=dbg_h["d_yb"][:], in_=ybT[:]),
                      waits=[("s", s_yb)], inc="gd", n=16)

            for pr in range(2):
                base = 44 + pr * 22
                h0s = sstat[:, base:base + 4].rearrange("p (t h) -> p t h", t=2)[:, :, 0:1]
                h1s = sstat[:, base:base + 4].rearrange("p (t h) -> p t h", t=2)[:, :, 1:2]
                ssq2 = sstat[:, base + 4:base + 6]
                s12 = sstat[:, base + 6:base + 8]
                m2 = sstat[:, base + 8:base + 10]
                t12 = sstat[:, base + 10:base + 12]
                nm22 = sstat[:, base + 12:base + 14]
                ve2 = sstat[:, base + 14:base + 16]
                std2 = sstat[:, base + 16:base + 18]
                rs2 = sstat[:, base + 18:base + 20]
                nmb2 = sstat[:, base + 20:base + 22]
                for ti in range(2):
                    tt = 2 * pr + ti
                    par = tt % 2
                    bank = ["pb2a", "pb2b"][par]
                    pby = [pb2a, pb2b][par]
                    for oc in range(8):
                        P.add("tensor", lambda e, tt=tt, oc=oc, pby=pby: e.matmul(
                            pby[:, oc * 128:(oc + 1) * 128],
                            ybT[:, oc * 512 + tt * 128: oc * 512 + (tt + 1) * 128], sid[:],
                            start=True, stop=True),
                            waits=([("s", s_yb)] + bg(bank)) if oc == 0 else [],
                            inc="te" if oc == 7 else None)
                    tev = P.ctr["te"]
                    v_e2 = 0
                    for hf in range(2):
                        acol = sstat[:, base + ti * 2 + hf: base + ti * 2 + hf + 1]
                        v_e2 = P.add("vector", lambda e, tt=tt, hf=hf, par=par, pby=pby,
                                     acol=acol: e.scalar_tensor_tensor(
                            xft[par][:].rearrange("p (o u) -> p o u", o=8)[
                                :, hf * 4:(hf + 1) * 4, :],
                            pby[:, hf * 512:(hf + 1) * 512].rearrange("p (a u) -> p a u", a=4),
                            1.0,
                            x1n[:, tt * 1024:(tt + 1) * 1024].rearrange(
                                "p (o u) -> p o u", o=8)[:, hf * 4:(hf + 1) * 4, :],
                            ALU.mult, ALU.add, accum_out=acol),
                            waits=[("te", tev), ("s", s_prev_apply2[par])], inc="v")
                    bset(bank, "v", v_e2)
                    vdrain()
                    P.add("vector", lambda e, par=par, ti=ti, ssq2=ssq2: e.scalar_tensor_tensor(
                        x1nb[:], xft[par][:], 1.0, xft[par][:], ALU.mult, ALU.mult,
                        accum_out=ssq2[:, ti:ti + 1]),
                        waits=bg("x1nb"), inc="v")
                vdrain()
                P.add("vector", lambda e, s12=s12, h0s=h0s, h1s=h1s: e.tensor_tensor(
                    s12, h0s, h1s, ALU.add), inc="v")
                P.add("vector", lambda e, t12=t12, ssq2=ssq2: e.tensor_scalar(
                    t12, ssq2, D * 1e-5, 1.0 / D, ALU.add, ALU.mult), inc="v")
                vdrain()
                P.add("vector", lambda e, m2=m2, s12=s12: e.tensor_scalar(
                    m2, s12, 1.0 / D, None, ALU.mult), inc="v")
                vdrain()
                P.add("vector", lambda e, nm22=nm22, m2=m2: e.scalar_tensor_tensor(
                    nm22, m2, -1.0, m2, ALU.mult, ALU.mult), inc="v")
                vdrain()
                P.add("vector", lambda e, ve2=ve2, t12=t12, nm22=nm22: e.tensor_tensor(
                    ve2, t12, nm22, ALU.add), inc="v")
                vv = vdrain(inc="v")
                P.add("scalar", lambda e, std2=std2, ve2=ve2: e.activation(
                    std2, ve2, AF.Sqrt, bias=0.0, scale=1.0),
                    waits=[("v", vv)], inc="s")
                sq = sdrain(inc="s")
                P.add("vector", lambda e, rs2=rs2, std2=std2: e.reciprocal(rs2, std2),
                      waits=[("s", sq)], inc="v")
                vdrain()
                P.add("vector", lambda e, nmb2=nmb2, m2=m2, rs2=rs2: e.scalar_tensor_tensor(
                    nmb2, m2, -1.0, rs2, ALU.mult, ALU.mult), inc="v")
                vnm = vdrain(inc="v")
                for ti in range(2):
                    tt = 2 * pr + ti
                    par = tt % 2
                    rsc = rs2[:, ti:ti + 1]
                    nmbc = nmb2[:, ti:ti + 1]
                    gtile = c * NTT + tt
                    P.add("scalar", lambda e, par=par, rsc=rsc, nmbc=nmbc: e.activation(
                        x1t[par][:], xft[par][:], AF.Identity, bias=nmbc, scale=rsc),
                        waits=[("v", vnm), ("st%d" % par, x1t_st_guard[par])],
                        inc="s")
                    s_ap2 = sdrain(inc="s")
                    s_prev_apply2[par] = s_ap2
                    v_xft_guard[par] = ("s", s_ap2)
                    x1t_st_guard[par] = P.add("sync", lambda e, gtile=gtile, par=par: e.dma_start(
                        out=y_h.rearrange("(n p) d -> n p d", p=128)[gtile],
                        in_=x1t[par][:]),
                        waits=[("s", s_ap2)], inc="st%d" % par, n=16)

        P.add("sync", lambda e: e.nop(),
              waits=[("st0", P.ctr["st0"]), ("st1", P.ctr["st1"])])

        with nc.Block() as block:
            @block.sync
            def _(eng):
                P.replay("sync", eng)

            @block.gpsimd
            def _(eng):
                P.replay("gpsimd", eng)

            @block.tensor
            def _(eng):
                P.replay("tensor", eng)

            @block.vector
            def _(eng):
                P.replay("vector", eng)

            @block.scalar
            def _(eng):
                P.replay("scalar", eng)

    return nc


def host_inputs(L0_attn_W_qkv, L0_attn_b_qkv, L0_attn_W_out, L0_attn_b_out,
                L0_ffn_kernel, L0_ffn_input_scale, L0_ffn_kernel_scale, L0_ffn_bias):
    Wq = np.asarray(L0_attn_W_qkv)
    bq = np.asarray(L0_attn_b_qkv)
    Wo = np.asarray(L0_attn_W_out)
    bo = np.asarray(L0_attn_b_out)
    kern = np.asarray(L0_ffn_kernel)
    bff = np.asarray(L0_ffn_bias)

    perm = []
    for i in range(16):
        perm += list(range(i * 192, i * 192 + 64))
        perm += list(range(i * 192 + 64, i * 192 + 128))
    for i in range(8):
        perm += list(range((2 * i) * 192 + 128, (2 * i) * 192 + 192))
        perm += list(range((2 * i + 1) * 192 + 128, (2 * i + 1) * 192 + 192))
    perm = np.array(perm)

    wqkv = np.ascontiguousarray(Wq[:, perm].reshape(8, 128, 3072))
    bqkv = np.ascontiguousarray(bq[perm].reshape(24, 128).T)
    wout = np.ascontiguousarray(np.stack(
        [np.concatenate([Wo[64 * j:64 * j + 64], Wo[64 * (j + 8):64 * (j + 8) + 64]],
                        axis=0) for j in range(8)]))
    kern_r = np.ascontiguousarray(kern.reshape(8, 128, 1024))
    bout = np.ascontiguousarray(bo.reshape(8, 128).T)
    bffn = np.ascontiguousarray(bff.reshape(8, 128).T)
    sin = np.ascontiguousarray(np.asarray(L0_ffn_input_scale).reshape(1, 1).astype(np.float32))
    sk = np.ascontiguousarray(np.asarray(L0_ffn_kernel_scale).reshape(1, 1).astype(np.float32))

    ident = np.eye(128, dtype=ml_dtypes.bfloat16)
    p8c = np.zeros((8, 128), dtype=ml_dtypes.bfloat16)
    for cc in range(8):
        p8c[cc, np.arange(128) % 8 == cc] = 8.0
    ones1 = np.ones((1, 128), dtype=np.float32)

    return dict(wqkv=wqkv, bqkv=bqkv, wout=wout, kern=kern_r, bout=bout,
                bffn=bffn, sin=sin, sk=sk, ident=ident, p8c=p8c, ones1=ones1)


def kernel(L0_attn_x, L0_attn_W_qkv, L0_attn_b_qkv, L0_attn_W_out, L0_attn_b_out,
           L0_ffn_kernel, L0_ffn_input_scale, L0_ffn_kernel_scale, L0_ffn_bias):
    x = np.ascontiguousarray(np.asarray(L0_attn_x).reshape(-1, D))
    ntok = x.shape[0]
    tpc = ntok // NCORE
    shared = host_inputs(L0_attn_W_qkv, L0_attn_b_qkv, L0_attn_W_out, L0_attn_b_out,
                         L0_ffn_kernel, L0_ffn_input_scale, L0_ffn_kernel_scale,
                         L0_ffn_bias)
    nc = build_program(tpc)
    in_maps = []
    for i in range(NCORE):
        m = dict(shared)
        m["x"] = np.ascontiguousarray(x[i * tpc:(i + 1) * tpc])
        in_maps.append(m)
    res = run_bass_kernel_spmd(nc, in_maps, core_ids=list(range(NCORE)))
    out = np.concatenate([np.asarray(r["y"]) for r in res.results], axis=0)
    return out.reshape(np.asarray(L0_attn_x).shape).astype(np.float32)
